# version 04
# baseline (speedup 1.0000x reference)
import numpy as np
import concourse.bass as bass
import concourse.mybir as mybir
from contextlib import ExitStack

F32 = mybir.dt.float32
F32R = mybir.dt.float32r
AF = mybir.ActivationFunctionType
ALU = mybir.AluOpType
AX = mybir.AxisListType

ENGS = ("sync", "tensor", "vector", "scalar", "gpsimd")


class Plan:

    def __init__(self):
        self.cnt = {}
        self.items = []
        self.sem_names = []

    def sem(self, name):
        if name not in self.cnt:
            self.cnt[name] = 0
            self.sem_names.append(name)
        return name

    def v(self, s):
        return self.cnt[s]

    def op(self, eng, fn, waits=(), incs=()):
        for s, n in incs:
            self.cnt[s] = self.cnt[s] + n
        self.items.append((eng, fn, tuple(waits), tuple(incs)))

    def emit(self, block, handles):
        for engname in ENGS:
            items = [(fn, w, i) for (e, fn, w, i) in self.items if e == engname]
            if not items:
                continue

            def section(q, _items=items, _eng=engname):
                guaranteed = {}
                for fn, waits, incs in _items:
                    for s, v in waits:
                        if guaranteed.get(s, 0) < v:
                            q.wait_ge(handles[s], v)
                            guaranteed[s] = v
                    ins = fn(q)
                    for s, n in incs:
                        ins.then_inc(handles[s], n)
                        guaranteed[s] = guaranteed.get(s, 0) + n if False else guaranteed.get(s, 0)

            getattr(block, engname)(section)


def shard_inputs(inputs, n_cores=8):
    d = inputs
    DIM, F, E, R, DEPTH = 1024, 4096, 8, 16, 6
    x = np.ascontiguousarray(np.asarray(d["L0_attn_x"], dtype=np.float32))
    B, T, _ = x.shape
    per = B // n_cores

    def f32c(a):
        return np.ascontiguousarray(np.asarray(a, dtype=np.float32))

    shared = {}
    shared["ident"] = np.eye(128, dtype=np.float32)
    shared["ones1"] = np.ones((1, 128), dtype=np.float32)
    shared["onescol"] = np.ones((128, 2), dtype=np.float32)
    for nm in ["wq", "wk", "wv", "wo", "w1", "w2"]:
        shared[nm] = f32c(d["L0_attn_" + nm])
    lncols = np.zeros((128, 8 * 4 * DEPTH), dtype=np.float32)
    for li in range(DEPTH):
        for ki, nm in enumerate(["ln1_w", "ln1_b", "ln2_w", "ln2_b"]):
            vcol = f32c(d["L0_attn_" + nm])[li]
            lncols[:, (li * 4 + ki) * 8 : (li * 4 + ki + 1) * 8] = vcol.reshape(8, 128).T
    shared["lncols"] = lncols
    shared["bo"] = f32c(d["L0_attn_bo"])
    shared["gate_wT"] = f32c(d["L0_ffn_gate_w"]).T.copy()
    shared["w_gateT"] = f32c(d["L0_ffn_w_gate"]).T.copy()
    shared["w_upT"] = f32c(d["L0_ffn_w_up"]).T.copy()
    shared["w_downT"] = f32c(d["L0_ffn_w_down"]).T.copy()
    A1, A3 = f32c(d["L0_ffn_A1"]), f32c(d["L0_ffn_A3"])
    B1, B3 = f32c(d["L0_ffn_B1"]), f32c(d["L0_ffn_B3"])
    A2, B2 = f32c(d["L0_ffn_A2"]), f32c(d["L0_ffn_B2"])
    a1cat = np.zeros((DIM, 384), dtype=np.float32)
    a3cat = np.zeros((DIM, 384), dtype=np.float32)
    for e in range(E):
        g, pos = divmod(e, 3)
        a1cat[:, 128 * g + 32 * pos : 128 * g + 32 * pos + 16] = A1[e].T
        a3cat[:, 128 * g + 32 * pos : 128 * g + 32 * pos + 16] = A3[e].T
    shared["A1cat"] = a1cat
    shared["A3cat"] = a3cat
    b1cat = np.zeros((3, 128, F), dtype=np.float32)
    b3cat = np.zeros((3, 128, F), dtype=np.float32)
    b2catT = np.zeros((128, DIM), dtype=np.float32)
    for e in range(E):
        g, pos = divmod(e, 3)
        b1cat[g, 32 * pos : 32 * pos + 16, :] = B1[e].T
        b3cat[g, 32 * pos : 32 * pos + 16, :] = B3[e].T
        b2catT[16 * e : 16 * e + 16, :] = B2[e].T
    shared["B1cat"] = b1cat
    shared["B3cat"] = b3cat
    shared["B2catT"] = b2catT
    a2T = np.zeros((F, 128), dtype=np.float32)
    for e in range(E):
        a2T[:, 16 * e : 16 * e + 16] = A2[e].T
    shared["A2T"] = a2T

    in_maps = []
    for c in range(n_cores):
        m = dict(shared)
        m["x_in"] = x[c * per : (c + 1) * per].reshape(per * T, DIM)
        in_maps.append(m)
    return in_maps


class MK:
    def __init__(self, cfg):
        self.cfg = cfg
        self.nc = bass.Bass("TRN2", target_bir_lowering=False)
        self.plan = Plan()
        self.ctx = ExitStack()
        self.handles = {}
        self.mmdt = F32R if cfg.get("f32r", True) else F32
        self.dev = cfg.get("dev", False)
        self.nlayers = cfg.get("nlayers", 6)
        self.T = cfg.get("T", 4096)
        self.nseq = cfg.get("nseq", 2)
        self.NTOK = self.T * self.nseq
        self.ntiles = self.NTOK // 128
        self.nt512 = self.NTOK // 512
        self.with_ffn = cfg.get("ffn", True)

    def sem(self, name):
        self.handles[name] = self.ctx.enter_context(self.nc.semaphore("s_" + name))
        return self.plan.sem(name)

    def sb(self, name, shape, dtype=F32):
        return self.ctx.enter_context(self.nc.sbuf_tensor(name, shape, dtype))

    def dram(self, name, shape, dtype=F32, kind="Internal"):
        if kind == "Internal" and self.dev:
            kind = "ExternalOutput"
        return self.nc.dram_tensor(name, shape, dtype, kind=kind).ap()

    def inp(self, name, shape, dtype=F32):
        return self.nc.dram_tensor(name, shape, dtype, kind="ExternalInput").ap()

    def build(self):
        nc, P = self.nc, self.plan
        mmdt = self.mmdt
        NTOK, ntiles, T = self.NTOK, self.ntiles, self.T
        tiles_per_seq = T // 128
        nl = self.nlayers

        x_in = self.inp("x_in", [NTOK, 1024])
        ident = self.inp("ident", [128, 128])
        ones1 = self.inp("ones1", [1, 128], mmdt)
        lncols = self.inp("lncols", [128, 8 * 4 * 6])
        W = {}
        for nm in ["wq", "wk", "wv", "wo"]:
            W[nm] = self.inp(nm, [6, 1024, 1024], mmdt)
        W["w1"] = self.inp("w1", [6, 1024, 4096], mmdt)
        W["w2"] = self.inp("w2", [6, 4096, 1024], mmdt)
        W["bo"] = self.inp("bo", [6, 1024], F32)
        W["onescol"] = self.inp("onescol", [128, 2], mmdt)
        W["gate_wT"] = self.inp("gate_wT", [1024, 8], mmdt)
        W["w_gateT"] = self.inp("w_gateT", [1024, 4096], mmdt)
        W["w_upT"] = self.inp("w_upT", [1024, 4096], mmdt)
        W["w_downT"] = self.inp("w_downT", [4096, 1024], mmdt)
        W["A1cat"] = self.inp("A1cat", [1024, 384], mmdt)
        W["A3cat"] = self.inp("A3cat", [1024, 384], mmdt)
        W["B1cat"] = self.inp("B1cat", [3, 128, 4096], mmdt)
        W["B3cat"] = self.inp("B3cat", [3, 128, 4096], mmdt)
        W["B2catT"] = self.inp("B2catT", [128, 1024], mmdt)
        W["A2T"] = self.inp("A2T", [4096, 128], mmdt)
        self.W = W

        y_out = self.dram("y_out", [NTOK, 1024], F32, kind="ExternalOutput")
        resid = self.dram("resid", [NTOK, 1024], F32)
        qsd = self.dram("qsd", [128, ntiles, 8, 128], mmdt)
        xn2fmd = self.dram("xn2fmd", [128, ntiles, 8, 128], mmdt)
        xtd = self.dram("xtd", [NTOK, 1024], F32)
        xtfmd = self.dram("xtfmd", [128, ntiles, 8, 128], mmdt)
        t1pd = self.dram("t1pd", [128, ntiles, 3, 128], mmdt)
        t3pd = self.dram("t3pd", [128, ntiles, 3, 128], mmdt)
        wgtd = self.dram("wgtd", [8, NTOK], F32)
        zd = self.dram("zd0", [128, NTOK], mmdt)
        outd = self.dram("outd", [NTOK, 1024], F32)
        self.D = dict(resid=resid, qsd=qsd, xn2fmd=xn2fmd, xtd=xtd, xtfmd=xtfmd,
                      t1pd=t1pd, t3pd=t3pd, wgtd=wgtd, zd=zd, outd=outd, y_out=y_out)
        if self.dev:
            self.D["ctxdump"] = self.dram("ctxdump", [128, self.nseq * 1024], F32)
            self.D["dbg"] = self.dram("dbg", [128, 8 * 1024], F32)

        WA = self.sb("WA", [128, 32768], mmdt)
        SMM = self.sb("SMM", [128, 12288], mmdt)
        SF32 = self.sb("SF32", [128, 6144])
        self.WAt, self.SMMt, self.SF32t = WA, SMM, SF32
        wb = WA
        xb = [SF32[:, 0:1024], SF32[:, 1024:2048]]
        xhat = SF32[:, 2048:3072]
        sqscr = SF32[:, 3072:4096]
        exps = SF32[:, 4096:5120]
        qs_tm = SF32[:, 5120:6144]
        xnfm = SMM[:, 0:1024]
        expk = SMM[:, 1024:2048]
        v_sb = SMM[:, 2048:3072]
        qs_fm = SMM[:, 3072:4096]
        ctx_sb = WA[:, 24576:24576 + self.nseq * 1024]
        sacc = [self.sb(f"sacc{i}", [128, 16]) for i in range(2)]
        lnsb = self.sb("lnsb", [128, 8 * 4 * 6])
        onesb = self.sb("onesb", [1, 128], mmdt)
        onecol = self.sb("onecol", [128, 2], mmdt)
        idsb = self.sb("idsb", [128, 128])
        eps_sb = self.sb("eps_sb", [128, 1])
        st1 = self.sb("st1", [128, 1])
        nm1 = self.sb("nm1", [128, 1])
        sq1 = self.sb("sq1", [128, 1])
        lv1 = self.sb("lv1", [128, 1])
        rstd1 = self.sb("rstd1", [128, 1])
        var1 = self.sb("var1", [128, 1])
        maxs = self.sb("maxs", [128, 8])
        ssum = self.sb("ssum", [128, 8])
        rsum = self.sb("rsum", [128, 8])
        ssum2 = self.sb("ssum2", [128, 8])
        rsum2 = self.sb("rsum2", [128, 8])
        rs2b = self.sb("rs2b", [128, 16])
        rs2 = self.sb("rs2", [128, 16])
        ps = self.ctx.enter_context(nc.psum_tensor("ps", [128, 4096], F32))
        self.sbufs = locals()

        R_TR = ps[:, 0:1024]
        R_K = ps[:, 1024:2048]
        R_Q = ps[:, 2048:3072]
        R_C = ps[:, 3072:4096]

        s = {n: self.sem(n) for n in
             ["DLX", "DLW", "DST", "TT", "VC", "SC", "GP"]}

        def OP(eng, fn, waits=(), incs=()):
            P.op(eng, fn, waits, incs)

        def mark(sem):
            return P.v(sem)

        OP("vector", lambda q: q.memset(eps_sb[:], 1e-5))
        OP("sync", lambda q: q.dma_start(out=lnsb[:], in_=lncols), incs=[(s["DLW"], 16)])
        OP("sync", lambda q: q.dma_start(out=onesb[:], in_=ones1), incs=[(s["DLW"], 16)])
        OP("sync", lambda q: q.dma_start(out=idsb[:], in_=ident), incs=[(s["DLW"], 16)])
        OP("sync", lambda q: q.dma_start(out=onecol[:], in_=W["onescol"]),
           incs=[(s["DLW"], 16)])
        init_w = mark(s["DLW"])

        def phase_A(l):
            bar = [(nm_, P.v(nm_)) for nm_ in s.values()]
            for wi, nm in enumerate(["wk", "wv", "wq"]):
                src = W[nm][l].rearrange("(c p) n -> p c n", p=128)
                dst = wb[:, wi * 8192 : (wi + 1) * 8192].rearrange("p (c n) -> p c n", n=1024)
                OP("sync", lambda q, src=src, dst=dst: q.dma_start(out=dst, in_=src),
                   waits=(bar if wi == 0 else []), incs=[(s["DLW"], 16)])
            wready = mark(s["DLW"])
            wkv = lambda wi, c, n0, n1: wb[:, wi * 8192 + c * 1024 + n0 : wi * 8192 + c * 1024 + n1]

            m_xfree = [(0, 0), (0, 0)]
            m_dump = 0
            m_prev = dict(VNF=0, SEXP=0, VV=0, VSD=0, SQE=0, VQF=0, VQS=0, VCD=0,
                          TK=0, TQ=0, TV=0, TTR=0, TTR2=0, TC8=0, TS8=0, VCTX=0, RSTD=0,
                          DQS=0)
            pend_tr2 = None

            for i in range(ntiles):
                seq = i // tiles_per_seq
                first = (i % tiles_per_seq) == 0
                last = (i % tiles_per_seq) == tiles_per_seq - 1
                xcur = xb[i % 2]
                src = x_in if l == 0 else resid

                OP("sync", lambda q, xcur=xcur, sl=src[i * 128:(i + 1) * 128, :]:
                   q.dma_start(out=xcur, in_=sl),
                   waits=[(s["VC"], m_xfree[i % 2][0]), (s["SC"], m_xfree[i % 2][1]),
                          (s["DST"], m_dump)] + (bar if i == 0 else []),
                   incs=[(s["DLX"], 16)])
                m_x = mark(s["DLX"])

                OP("vector", lambda q, xcur=xcur: q.reduce_sum(st1[:], xcur, axis=AX.X,
                                                               negate=True),
                   waits=[(s["DLX"], m_x), (s["SC"], m_prev["RSTD"]),
                          (s["DST"], m_dump)], incs=[(s["VC"], 1)])
                m_sum = mark(s["VC"])
                OP("scalar", lambda q: q.mul(nm1[:], st1[:], 1.0 / 1024),
                   waits=[(s["VC"], m_sum), (s["DST"], m_dump)])
                OP("scalar", lambda q, xcur=xcur: q.activation(
                    sqscr, xcur, AF.Square, bias=st1[:], scale=1024.0,
                    accum_out=sq1[:]))
                OP("scalar", lambda q: q.activation(lv1[:], sq1[:], AF.Ln, bias=eps_sb[:],
                                                    scale=1.0 / (1024.0 ** 3)))
                OP("scalar", lambda q: q.activation(rstd1[:], lv1[:], AF.Exp, scale=-0.5),
                   incs=[(s["SC"], 1)])
                m_rstd = mark(s["SC"])
                m_prev["RSTD"] = m_rstd
                OP("vector", lambda q, xcur=xcur: q.tensor_scalar(
                    xhat, xcur, nm1[:], rstd1[:], ALU.add, ALU.mult),
                   waits=[(s["SC"], m_rstd), (s["TT"], m_prev["TTR"])], incs=[(s["VC"], 1)])
                m_xhat = mark(s["VC"])
                if self.dev and i == 0 and l == 0:
                    OP("gpsimd", lambda q: q.dma_start(out=self.D["dbg"][:, 0:1024], in_=xhat),
                       waits=[(s["VC"], m_xhat)], incs=[(s["DST"], 16)])
                    OP("gpsimd", lambda q, xcur=xcur: q.dma_start(out=self.D["dbg"][:, 6144:7168], in_=xcur),
                       waits=[(s["VC"], m_xhat)], incs=[(s["DST"], 16)])
                    OP("gpsimd", lambda q: q.dma_start(out=self.D["dbg"][:, 7168:7169], in_=st1[:], allow_slow_non_contiguous=True),
                       waits=[(s["VC"], m_xhat)], incs=[(s["DST"], 16)])
                    OP("gpsimd", lambda q: q.dma_start(out=self.D["dbg"][:, 7169:7170], in_=nm1[:], allow_slow_non_contiguous=True),
                       waits=[(s["VC"], m_xhat)], incs=[(s["DST"], 16)])
                    OP("gpsimd", lambda q: q.dma_start(out=self.D["dbg"][:, 7170:7171], in_=rstd1[:], allow_slow_non_contiguous=True),
                       waits=[(s["VC"], m_xhat)], incs=[(s["DST"], 16)])

                for c in range(8):
                    OP("tensor", lambda q, c=c: q.transpose(
                        R_TR[:, c * 128:(c + 1) * 128], xhat[:, c * 128:(c + 1) * 128], idsb[:]),
                       waits=([(s["VC"], m_xhat), (s["VC"], m_prev["VSD"]),
                               (s["DLW"], wready)] if c == 0 else []),
                       incs=([(s["TT"], 1)] if c == 7 else []))
                m_prev["TTR"] = mark(s["TT"])
                for c in range(8):
                    wc = lnsb[:, (l * 4 + 0) * 8 + c : (l * 4 + 0) * 8 + c + 1]
                    bc = lnsb[:, (l * 4 + 1) * 8 + c : (l * 4 + 1) * 8 + c + 1]
                    OP("vector", lambda q, c=c, wc=wc, bc=bc: q.tensor_scalar(
                        xnfm[:, c * 128:(c + 1) * 128], R_TR[:, c * 128:(c + 1) * 128],
                        wc, bc, ALU.mult, ALU.add),
                       waits=([(s["TT"], m_prev["TTR"]), (s["TT"], m_prev["TQ"])] if c == 0 else []),
                       incs=([(s["VC"], 1)] if c == 7 else []))
                m_prev["VNF"] = mark(s["VC"])
                if self.dev and i == 0 and l == 0:
                    OP("gpsimd", lambda q: q.dma_start(out=self.D["dbg"][:, 1024:2048].bitcast(mmdt), in_=xnfm),
                       waits=[(s["VC"], m_prev["VNF"])], incs=[(s["DST"], 16)])

                if pend_tr2 is not None:
                    pend_tr2()
                    pend_tr2 = None

                for n in range(2):
                    for c in range(8):
                        OP("tensor", lambda q, c=c, n=n: q.matmul(
                            R_K[:, n * 512:(n + 1) * 512], xnfm[:, c * 128:(c + 1) * 128],
                            wkv(0, c, n * 512, (n + 1) * 512),
                            start=(c == 0), stop=(c == 7)),
                           waits=([(s["VC"], m_prev["VNF"]), (s["SC"], m_prev["VQF"])]
                                  if (n == 0 and c == 0) else []),
                           incs=([(s["TT"], 1)] if (n == 1 and c == 7) else []))
                m_prev["TK"] = mark(s["TT"])
                OP("scalar", lambda q: q.activation(expk, ps[:, 1024:2048], AF.Exp),
                   waits=[(s["TT"], m_prev["TK"])], incs=[(s["SC"], 1)])
                m_prev["SEXP"] = mark(s["SC"])
                if self.dev and i == 0 and l == 0:
                    OP("gpsimd", lambda q: q.dma_start(out=self.D["dbg"][:, 2048:3072].bitcast(mmdt), in_=expk),
                       waits=[(s["SC"], m_prev["SEXP"])], incs=[(s["DST"], 16)])

                for n in range(2):
                    for c in range(8):
                        OP("tensor", lambda q, c=c, n=n: q.matmul(
                            R_TR[:, n * 512:(n + 1) * 512], xnfm[:, c * 128:(c + 1) * 128],
                            wkv(1, c, n * 512, (n + 1) * 512),
                            start=(c == 0), stop=(c == 7)),
                           waits=([(s["VC"], m_prev["VNF"]), (s["VC"], m_prev["VV"])]
                                  if (n == 0 and c == 0) else []),
                           incs=([(s["TT"], 1)] if (n == 1 and c == 7) else []))
                m_prev["TV"] = mark(s["TT"])
                OP("vector", lambda q: q.tensor_copy(v_sb, ps[:, 0:1024]),
                   waits=[(s["TT"], m_prev["TV"])], incs=[(s["VC"], 1)])
                m_prev["VV"] = mark(s["VC"])
                if self.dev and i == 0 and l == 0:
                    OP("gpsimd", lambda q: q.dma_start(out=self.D["dbg"][:, 3072:4096].bitcast(mmdt), in_=v_sb),
                       waits=[(s["VC"], m_prev["VV"])], incs=[(s["DST"], 16)])

                for n in range(2):
                    for c in range(8):
                        OP("tensor", lambda q, c=c, n=n: q.matmul(
                            R_Q[:, n * 512:(n + 1) * 512], xnfm[:, c * 128:(c + 1) * 128],
                            wkv(2, c, n * 512, (n + 1) * 512),
                            start=(c == 0), stop=(c == 7)),
                           waits=([(s["SC"], m_prev["SQE"])] if (n == 0 and c == 0) else []),
                           incs=([(s["TT"], 1)] if (n == 1 and c == 7) else []))
                m_prev["TQ"] = mark(s["TT"])

                for h in range(8):
                    OP("tensor", lambda q, h=h, first=first, last=last: q.matmul(
                        R_C[:, h * 128:(h + 1) * 128], expk[:, h * 128:(h + 1) * 128],
                        v_sb[:, h * 128:(h + 1) * 128],
                        start=(first and h % 4 == 0), stop=last, skip_group_check=True),
                       waits=([(s["SC"], m_prev["SEXP"]), (s["VC"], m_prev["VV"]),
                               (s["VC"], m_prev["VCTX"])] if h == 0 else []),
                       incs=([(s["TT"], 1)] if h == 7 else []))
                m_prev["TC8"] = mark(s["TT"])
                for h in range(8):
                    OP("tensor", lambda q, h=h: q.matmul(
                        R_TR[:, h * 2:h * 2 + 2], expk[:, h * 128:(h + 1) * 128], onecol[:],
                        start=True, stop=True),
                       waits=([(s["VC"], m_prev["VV"])] if h == 0 else []),
                       incs=([(s["TT"], 1)] if h == 7 else []))
                m_prev["TS8"] = mark(s["TT"])
                par = (i % tiles_per_seq) % 2
                if first:
                    OP("vector", lambda q, par=par: q.tensor_copy(sacc[par][:], ps[:, 0:16]),
                       waits=[(s["TT"], m_prev["TS8"])], incs=[(s["VC"], 1)])
                else:
                    OP("vector", lambda q, par=par: q.tensor_tensor(
                        sacc[par][:], sacc[1 - par][:], ps[:, 0:16], ALU.add),
                       waits=[(s["TT"], m_prev["TS8"])], incs=[(s["VC"], 1)])
                m_prev["VSD"] = mark(s["VC"])

                q3 = R_Q.rearrange("p (h t) -> p h t", t=128)
                OP("vector", lambda q, q3=q3: q.tensor_reduce(
                    maxs[:], q3, axis=AX.X, op=ALU.max, negate=True),
                   waits=[(s["TT"], m_prev["TQ"]), (s["SC"], m_prev["SQE"])],
                   incs=[(s["VC"], 1)])
                m_qm = mark(s["VC"])
                for h in range(8):
                    OP("scalar", lambda q, h=h: q.activation(
                        exps[:, h * 128:(h + 1) * 128], R_Q[:, h * 128:(h + 1) * 128],
                        AF.Exp, bias=maxs[:, h:h + 1], accum_out=ssum[:, h:h + 1]),
                       waits=([(s["VC"], m_qm)] if h == 0 else []),
                       incs=([(s["SC"], 1)] if h == 7 else []))
                OP("scalar", lambda q: q.copy(ssum2[:], ssum[:]), incs=[(s["SC"], 1)])
                m_prev["SQE"] = mark(s["SC"])
                OP("vector", lambda q: q.reciprocal(rsum[:], ssum2[:]),
                   waits=[(s["SC"], m_prev["SQE"])], incs=[(s["VC"], 1)])
                m_rr = mark(s["VC"])
                OP("scalar", lambda q: q.copy(rsum2[:], rsum[:]),
                   waits=[(s["VC"], m_rr)], incs=[(s["SC"], 1)])
                m_rr2 = mark(s["SC"])
                for h in range(8):
                    OP("vector", lambda q, h=h: q.tensor_scalar(
                        qs_tm[:, h * 128:(h + 1) * 128], exps[:, h * 128:(h + 1) * 128],
                        rsum2[:, h:h + 1], None, ALU.mult),
                       waits=([(s["TT"], m_prev["TTR2"]), (s["SC"], m_rr2)] if h == 0 else []),
                       incs=([(s["VC"], 1)] if h == 7 else []))
                m_prev["VQS"] = mark(s["VC"])
                if self.dev and i == 0 and l == 0:
                    OP("gpsimd", lambda q: q.dma_start(out=self.D["dbg"][:, 4096:5120], in_=qs_tm),
                       waits=[(s["VC"], m_prev["VQS"])], incs=[(s["DST"], 16)])
                    OP("gpsimd", lambda q: q.dma_start(out=self.D["dbg"][:, 5120:6144], in_=exps),
                       waits=[(s["VC"], m_prev["VQS"])], incs=[(s["DST"], 16)])
                    m_dump = mark(s["DST"])

                def make_tr2(i=i, vqs=m_prev["VQS"], tk_cur=m_prev["TK"]):
                    def emit():
                        for c in range(8):
                            OP("tensor", lambda q, c=c: q.transpose(
                                R_K[:, c * 128:(c + 1) * 128], qs_tm[:, c * 128:(c + 1) * 128],
                                idsb[:]),
                               waits=([(s["VC"], vqs)] if c == 0 else []),
                               incs=([(s["TT"], 1)] if c == 7 else []))
                        m_tr2 = mark(s["TT"])
                        m_prev["TTR2"] = m_tr2
                        for c in range(8):
                            OP("scalar", lambda q, c=c: q.copy(
                                qs_fm[:, c * 128:(c + 1) * 128], R_K[:, c * 128:(c + 1) * 128]),
                               waits=([(s["TT"], m_tr2), (s["DST"], m_prev["DQS"])]
                                      if c == 0 else []),
                               incs=([(s["SC"], 1)] if c == 7 else []))
                        m_prev["VQF"] = mark(s["SC"])
                        OP("gpsimd", lambda q, i=i: q.dma_start(
                            out=qsd[:, i], in_=qs_fm.rearrange("p (c t) -> p c t", t=128)),
                           waits=[(s["SC"], m_prev["VQF"])], incs=[(s["DST"], 16)])
                        m_prev["DQS"] = mark(s["DST"])
                    return emit
                pend_tr2 = make_tr2()

                m_xfree[i % 2] = (m_xhat, m_rstd)

                if last:
                    pend_tr2()
                    pend_tr2 = None
                    OP("vector", lambda q, par=par: q.reciprocal(rs2[:], sacc[par][:]),
                       waits=[(s["TT"], m_prev["TC8"])], incs=[(s["VC"], 1)])
                    m_rc = mark(s["VC"])
                    OP("scalar", lambda q: q.copy(rs2b[:], rs2[:]),
                       waits=[(s["VC"], m_rc)], incs=[(s["SC"], 1)])
                    m_rc2 = mark(s["SC"])
                    for h in range(8):
                        OP("vector", lambda q, h=h, seq=seq: q.tensor_scalar(
                            ctx_sb[:, seq * 1024 + h * 128: seq * 1024 + (h + 1) * 128],
                            R_C[:, h * 128:(h + 1) * 128], rs2b[:, 2 * h:2 * h + 1], None, ALU.mult),
                           waits=([(s["SC"], m_rc2)] if h == 0 else []),
                           incs=([(s["VC"], 1)] if h == 7 else []))
                    m_prev["VCTX"] = mark(s["VC"])
            if self.dev:
                OP("gpsimd", lambda q: q.dma_start(
                    out=self.D["ctxdump"], in_=ctx_sb),
                   waits=[(s["VC"], m_prev["VCTX"])], incs=[(s["DST"], 16)])
            self.s = s
            return m_prev

        self.phase_A = phase_A

        def phase_B(l):
            bar = [(nm_, P.v(nm_)) for nm_ in s.values()]
            woW = WA[:, 0:8192]
            xn2out = WA[:, 8192:9216]
            qsb = SMM[:, 0:4096]
            attnfm = SMM[:, 4096:8192]
            xm = [SF32[:, 0:1024], SF32[:, 1024:2048]]
            x2m = [SF32[:, 2048:3072], SF32[:, 3072:4096]]
            sqB = SF32[:, 4096:5120]
            bo_b = SF32[:, 5120:6144]
            pA = lambda h: ps[:, (h % 2) * 512:(h % 2) * 512 + 512]
            pY = lambda m: ps[:, 1024 + (m % 2) * 1024: 1024 + (m % 2) * 1024 + 1024]
            pT = ps[:, 3072:4096]
            src = x_in if l == 0 else resid

            OP("sync", lambda q: q.dma_start(
                out=woW.rearrange("p (c n) -> p c n", n=1024),
                in_=W["wo"][l].rearrange("(c p) n -> p c n", p=128)),
               waits=bar, incs=[(s["DLW"], 16)])
            OP("sync", lambda q: q.dma_start(
                out=bo_b, in_=W["bo"][l:l + 1, :].to_broadcast([128, 1024])),
               incs=[(s["DLW"], 16)])
            m_w = mark(s["DLW"])

            mk = dict(yTT=0, aSC=0, stVC=[0, 0], trTT=[0, 0], DQS=0, DX2=[0, 0],
                      x2VC=[0, 0], x2SC=[0, 0], TRD=0, RSTD=0)
            first_v = [True]
            first_s = [True]

            for j in range(NTOK // 512):
                for c_ in range(8):
                    OP("sync", lambda q, j=j, c_=c_: q.dma_start(
                        out=qsb[:, c_ * 512:(c_ + 1) * 512].rearrange("p (t k) -> p t k", k=128),
                        in_=qsd[:, 4 * j:4 * j + 4, c_, :]),
                       waits=(bar + [(s["TT"], mk["yTT"])] if c_ == 0 else []),
                       incs=[(s["DLX"], 16)])
                m_qs = mark(s["DLX"])
                seq = (j * 512) // T
                evac_marks = {}
                for h in range(8):
                    w_h = [(s["DLX"], m_qs), (s["DLW"], m_w)] if h == 0 else (
                        [(s["SC"], evac_marks[h - 2])] if h >= 2 else [])
                    OP("tensor", lambda q, h=h, seq=seq: q.matmul(
                        pA(h), ctx_sb[:, seq * 1024 + h * 128:seq * 1024 + (h + 1) * 128],
                        qsb[:, h * 512:(h + 1) * 512], start=True, stop=True),
                       waits=w_h, incs=[(s["TT"], 1)])
                    m_mm = mark(s["TT"])
                    OP("scalar", lambda q, h=h: q.copy(
                        attnfm[:, h * 512:(h + 1) * 512], pA(h)),
                       waits=[(s["TT"], m_mm)] + (bar if first_s[0] else []),
                       incs=[(s["SC"], 1)])
                    first_s[0] = False
                    evac_marks[h] = mark(s["SC"])
                m_attn = mark(s["SC"])

                for m in range(4):
                    i128 = 4 * j + m
                    OP("sync", lambda q, m=m, i128=i128: q.dma_start(
                        out=xm[m % 2], in_=src[i128 * 128:(i128 + 1) * 128, :]),
                       waits=[(s["TT"], mk["trTT"][m % 2])], incs=[(s["DLX"], 16)])
                    m_x = mark(s["DLX"])
                    for n in range(2):
                        for h in range(8):
                            OP("tensor", lambda q, m=m, n=n, h=h: q.matmul(
                                pY(m)[:, n * 512:(n + 1) * 512],
                                attnfm[:, h * 512 + m * 128: h * 512 + (m + 1) * 128],
                                woW[:, h * 1024 + n * 512: h * 1024 + (n + 1) * 512],
                                start=(h == 0), stop=(h == 7)),
                               waits=([(s["SC"], m_attn), (s["VC"], mk["x2VC"][m % 2])]
                                      if (n == 0 and h == 0) else []),
                               incs=([(s["TT"], 1)] if (n == 1 and h == 7) else []))
                    mk["yTT"] = mark(s["TT"])
                    OP("vector", lambda q, m=m: q.tensor_tensor(sqB, pY(m), bo_b, ALU.add),
                       waits=[(s["TT"], mk["yTT"]), (s["DLX"], m_x),
                              (s["SC"], mk["x2SC"][m % 2]), (s["DST"], mk["DX2"][m % 2])]
                             + (bar if first_v[0] else []))
                    first_v[0] = False
                    OP("vector", lambda q, m=m: q.tensor_tensor(
                        x2m[m % 2], sqB, xm[m % 2], ALU.add), incs=[(s["VC"], 1)])
                    mk["x2VC"][m % 2] = mark(s["VC"])
                    OP("gpsimd", lambda q, m=m, i128=i128: q.dma_start(
                        out=resid[i128 * 128:(i128 + 1) * 128, :], in_=x2m[m % 2]),
                       waits=[(s["VC"], mk["x2VC"][m % 2])], incs=[(s["DST"], 16)])
                    mk["DX2"][m % 2] = mark(s["DST"])
                    OP("vector", lambda q, m=m: q.reduce_sum(
                        st1[:], x2m[m % 2], axis=AX.X, negate=True),
                       waits=[(s["SC"], mk["RSTD"])], incs=[(s["VC"], 1)])
                    m_sum = mark(s["VC"])
                    OP("scalar", lambda q: q.mul(nm1[:], st1[:], 1.0 / 1024),
                       waits=[(s["VC"], m_sum)])
                    OP("scalar", lambda q, m=m: q.activation(
                        sqB, x2m[m % 2], AF.Square, bias=st1[:], scale=1024.0,
                        accum_out=sq1[:]))
                    OP("scalar", lambda q: q.activation(lv1[:], sq1[:], AF.Ln, bias=eps_sb[:],
                                                        scale=1.0 / (1024.0 ** 3)))
                    OP("scalar", lambda q: q.activation(rstd1[:], lv1[:], AF.Exp, scale=-0.5),
                       incs=[(s["SC"], 1)])
                    mk["RSTD"] = mark(s["SC"])
                    mk["x2SC"][m % 2] = mk["RSTD"]
                    OP("vector", lambda q, m=m: q.tensor_scalar(
                        xm[m % 2], x2m[m % 2], nm1[:], rstd1[:], ALU.add, ALU.mult),
                       waits=[(s["SC"], mk["RSTD"]), (s["TT"], mk["trTT"][m % 2])],
                       incs=[(s["VC"], 1)])
                    m_hat = mark(s["VC"])
                    for c in range(8):
                        OP("tensor", lambda q, c=c, m=m: q.transpose(
                            pT[:, c * 128:(c + 1) * 128], xm[m % 2][:, c * 128:(c + 1) * 128],
                            idsb[:]),
                           waits=([(s["VC"], m_hat), (s["VC"], mk["TRD"])] if c == 0 else []),
                           incs=([(s["TT"], 1)] if c == 7 else []))
                    mk["trTT"][m % 2] = mark(s["TT"])
                    for c in range(8):
                        wc = lnsb[:, (l * 4 + 2) * 8 + c:(l * 4 + 2) * 8 + c + 1]
                        bc = lnsb[:, (l * 4 + 3) * 8 + c:(l * 4 + 3) * 8 + c + 1]
                        OP("vector", lambda q, c=c, wc=wc, bc=bc: q.tensor_scalar(
                            xn2out[:, c * 128:(c + 1) * 128], pT[:, c * 128:(c + 1) * 128],
                            wc, bc, ALU.mult, ALU.add),
                           waits=([(s["TT"], mk["trTT"][m % 2]), (s["DST"], mk["DQS"])]
                                  if c == 0 else []),
                           incs=([(s["VC"], 1)] if c == 7 else []))
                    mk["TRD"] = mark(s["VC"])
                    OP("gpsimd", lambda q, i128=i128: q.dma_start(
                        out=xn2fmd[:, i128], in_=xn2out.rearrange("p (c t) -> p c t", t=128)),
                       waits=[(s["VC"], mk["TRD"])], incs=[(s["DST"], 16)])
                    mk["DQS"] = mark(s["DST"])

        self.phase_B = phase_B

        def phase_C(l, F0):
            bar = [(nm_, P.v(nm_)) for nm_ in s.values()]
            w1h = WA[:, 0:16384]
            w2h = WA[:, 16384:32768]
            xn2sb = SMM[:, 0:4096]
            hdnfm = SMM[:, 4096:12288]
            tl = SF32[:, 0:512]
            stage = [SF32[:, 1024:2048], SF32[:, 2048:3072]]
            pH = lambda fs: ps[:, (fs % 4) * 512:(fs % 4) * 512 + 512]
            pY2 = lambda m_: ps[:, 2048 + (m_ % 2) * 1024: 2048 + (m_ % 2) * 1024 + 1024]

            OP("sync", lambda q: q.dma_start(
                out=w1h.rearrange("p (c n) -> p c n", n=2048),
                in_=W["w1"][l, :, F0:F0 + 2048].rearrange("(c p) n -> p c n", p=128)),
               waits=bar, incs=[(s["DLW"], 16)])
            OP("sync", lambda q: q.dma_start(
                out=w2h.rearrange("p (k n) -> p k n", n=1024),
                in_=W["w2"][l, F0:F0 + 2048, :].rearrange("(k p) n -> p k n", p=128)),
               incs=[(s["DLW"], 16)])
            m_w = mark(s["DLW"])

            mk = dict(w2TT=0, lkVC=[0, 0, 0, 0], drSC=[0, 0], drDST=[0, 0], hTT={})
            firsts = {"vector": [bar], "scalar": [bar], "gpsimd": [bar]}

            def fw(eng):
                if firsts.get(eng) and firsts[eng]:
                    return firsts[eng].pop()
                return []

            for j in range(NTOK // 512):
                for c_ in range(8):
                    OP("sync", lambda q, j=j, c_=c_: q.dma_start(
                        out=xn2sb[:, c_ * 512:(c_ + 1) * 512].rearrange("p (t k) -> p t k", k=128),
                        in_=xn2fmd[:, 4 * j:4 * j + 4, c_, :]),
                       waits=([(s["TT"], mk["hTT"].get(15, 0))] if c_ == 0 else []),
                       incs=[(s["DLX"], 16)])
                m_xn = mark(s["DLX"])
                for fs in range(16):
                    for c_ in range(8):
                        OP("tensor", lambda q, fs=fs, c_=c_: q.matmul(
                            pH(fs), w1h[:, c_ * 2048 + fs * 128: c_ * 2048 + (fs + 1) * 128],
                            xn2sb[:, c_ * 512:(c_ + 1) * 512],
                            start=(c_ == 0), stop=(c_ == 7)),
                           waits=([(s["DLX"], m_xn), (s["DLW"], m_w),
                                   (s["VC"], mk["lkVC"][fs % 4])] if c_ == 0 else []),
                           incs=([(s["TT"], 1)] if c_ == 7 else []))
                    m_h = mark(s["TT"])
                    mk["hTT"][fs] = m_h
                    OP("scalar", lambda q, fs=fs: q.mul(tl, pH(fs), 0.01),
                       waits=[(s["TT"], m_h)] + fw("scalar"), incs=[(s["SC"], 1)])
                    m_tl = mark(s["SC"])
                    OP("vector", lambda q, fs=fs: q.tensor_tensor(
                        hdnfm[:, fs * 512:(fs + 1) * 512], pH(fs), tl, ALU.max),
                       waits=[(s["SC"], m_tl), (s["TT"], mk["w2TT"])] + fw("vector"),
                       incs=[(s["VC"], 1)])
                    mk["lkVC"][fs % 4] = mark(s["VC"])
                m_leaky = mark(s["VC"])
                for m_ in range(4):
                    i128 = 4 * j + m_
                    for n in range(2):
                        for k_ in range(16):
                            OP("tensor", lambda q, m_=m_, n=n, k_=k_: q.matmul(
                                pY2(m_)[:, n * 512:(n + 1) * 512],
                                hdnfm[:, k_ * 512 + m_ * 128: k_ * 512 + (m_ + 1) * 128],
                                w2h[:, k_ * 1024 + n * 512: k_ * 1024 + (n + 1) * 512],
                                start=(k_ == 0), stop=(k_ == 15)),
                               waits=([(s["VC"], m_leaky), (s["SC"], mk["drSC"][m_ % 2])]
                                      if (n == 0 and k_ == 0) else []),
                               incs=([(s["TT"], 1)] if (n == 1 and k_ == 15) else []))
                    m_y2 = mark(s["TT"])
                    if m_ == 3:
                        mk["w2TT"] = m_y2
                    OP("scalar", lambda q, m_=m_: q.copy(stage[m_ % 2], pY2(m_)),
                       waits=[(s["TT"], m_y2), (s["DST"], mk["drDST"][m_ % 2])],
                       incs=[(s["SC"], 1)])
                    mk["drSC"][m_ % 2] = mark(s["SC"])
                    OP("gpsimd", lambda q, m_=m_, i128=i128: q.dma_start(
                        out=resid[i128 * 128:(i128 + 1) * 128, :], in_=stage[m_ % 2],
                        accum_op=ALU.add),
                       waits=[(s["SC"], mk["drSC"][m_ % 2])] + fw("gpsimd"),
                       incs=[(s["DST"], 16)])
                    mk["drDST"][m_ % 2] = mark(s["DST"])

        self.phase_C = phase_C

        def phase_T():
            bar = [(nm_, P.v(nm_)) for nm_ in s.values()]
            a1W = WA[:, 0:3072]
            a3W = WA[:, 3072:6144]
            gW = WA[:, 6144:6208]
            xtout = WA[:, 6656:7680]
            t1out = WA[:, 7680:8064]
            t3out = WA[:, 8064:8448]
            wfm = WA[:, 8448:8704]
            xm = [SF32[:, 0:1024], SF32[:, 1024:2048]]
            x2m = [SF32[:, 2048:3072], SF32[:, 3072:4096]]
            wpad = SF32[:, 4096:4352]
            t1s = SF32[:, 4608:4992]
            t3s = SF32[:, 5632:6016]
            top8 = SF32[:, 5120:5128]
            top8b = SF32[:, 5128:5136]
            e8 = SF32[:, 5136:5144]
            msk = SF32[:, 5144:5152]
            wnum = SF32[:, 5152:5160]
            w_tm = SF32[:, 5160:5168]
            negv1 = SF32[:, 5168:5169]
            ev2 = SF32[:, 5169:5170]
            den = SF32[:, 5176:5184]
            rden = SF32[:, 5184:5192]
            rdenb = SF32[:, 5192:5200]
            pT = ps[:, 0:1024]
            pG = ps[:, 1024:1032]
            pt1 = ps[:, 1536:1920]
            pt3 = ps[:, 2560:2944]
            pW = ps[:, 2048:2560]

            OP("sync", lambda q: q.dma_start(
                out=a1W.rearrange("p (c n) -> p c n", n=384),
                in_=W["A1cat"].rearrange("(c p) n -> p c n", p=128)),
               waits=bar, incs=[(s["DLW"], 16)])
            OP("sync", lambda q: q.dma_start(
                out=a3W.rearrange("p (c n) -> p c n", n=384),
                in_=W["A3cat"].rearrange("(c p) n -> p c n", p=128)),
               incs=[(s["DLW"], 16)])
            OP("sync", lambda q: q.dma_start(
                out=gW.rearrange("p (c n) -> p c n", n=8),
                in_=W["gate_wT"].rearrange("(c p) n -> p c n", p=128)),
               incs=[(s["DLW"], 16)])
            m_w = mark(s["DLW"])
            OP("vector", lambda q: q.memset(wpad, 0.0), waits=bar)
            first_s = [bar]
            first_g = [bar]
            mk = dict(RSTD=0, trTT=[0, 0], evSC=0, DXT=0, mmTT=0, WEV=0, T2=0, MSK=0, WT=0,
                      xD=[0, 0], x2V=[0, 0], x2S=[0, 0])

            for i128 in range(ntiles):
                m = i128 % 2
                OP("sync", lambda q, i128=i128, m=m: q.dma_start(
                    out=xm[m], in_=resid[i128 * 128:(i128 + 1) * 128, :]),
                   waits=[(s["TT"], mk["trTT"][m]), (s["DST"], mk["xD"][m])]
                         + (bar if i128 == 0 else []),
                   incs=[(s["DLX"], 16)])
                OP("sync", lambda q, i128=i128, m=m: q.dma_start(
                    out=x2m[m], in_=x_in[i128 * 128:(i128 + 1) * 128, :]),
                   waits=[(s["VC"], mk["x2V"][m]), (s["SC"], mk["x2S"][m])],
                   incs=[(s["DLX"], 16)])
                m_x = mark(s["DLX"])
                OP("vector", lambda q, m=m: q.tensor_tensor(x2m[m], xm[m], x2m[m], ALU.add),
                   waits=[(s["DLX"], m_x), (s["SC"], mk["RSTD"])], incs=[(s["VC"], 1)])
                OP("vector", lambda q, m=m: q.reduce_sum(st1[:], x2m[m], axis=AX.X,
                                                         negate=True), incs=[(s["VC"], 1)])
                m_sum = mark(s["VC"])
                OP("scalar", lambda q: q.mul(nm1[:], st1[:], 1.0 / 1024),
                   waits=[(s["VC"], m_sum)] + (first_s.pop() if first_s else []))
                OP("scalar", lambda q, m=m: q.activation(
                    xm[m], x2m[m], AF.Square, bias=st1[:], scale=1024.0, accum_out=sq1[:]))
                OP("scalar", lambda q: q.activation(lv1[:], sq1[:], AF.Ln, bias=eps_sb[:],
                                                    scale=1.0 / (1024.0 ** 3)))
                OP("scalar", lambda q: q.activation(rstd1[:], lv1[:], AF.Exp, scale=-0.5),
                   incs=[(s["SC"], 1)])
                mk["RSTD"] = mark(s["SC"])
                mk["x2S"][m] = mk["RSTD"]
                OP("vector", lambda q, m=m: q.tensor_scalar(
                    xm[m], x2m[m], nm1[:], rstd1[:], ALU.add, ALU.mult),
                   waits=[(s["SC"], mk["RSTD"]), (s["TT"], mk["trTT"][m])],
                   incs=[(s["VC"], 1)])
                m_hat = mark(s["VC"])
                mk["x2V"][m] = m_hat
                OP("gpsimd", lambda q, i128=i128, m=m: q.dma_start(
                    out=xtd[i128 * 128:(i128 + 1) * 128, :], in_=xm[m]),
                   waits=[(s["VC"], m_hat)] + (first_g.pop() if first_g else []),
                   incs=[(s["DST"], 16)])
                mk["xD"][m] = mark(s["DST"])
                for c in range(8):
                    OP("tensor", lambda q, c=c, m=m: q.transpose(
                        pT[:, c * 128:(c + 1) * 128], xm[m][:, c * 128:(c + 1) * 128], idsb[:]),
                       waits=([(s["VC"], m_hat), (s["SC"], mk["T2"]),
                               (s["DLW"], m_w)] if c == 0 else []),
                       incs=([(s["TT"], 1)] if c == 7 else []))
                mk["trTT"][m] = mark(s["TT"])
                for c in range(8):
                    OP("scalar", lambda q, c=c: q.copy(
                        xtout[:, c * 128:(c + 1) * 128], pT[:, c * 128:(c + 1) * 128]),
                       waits=([(s["TT"], mk["trTT"][m]), (s["DST"], mk["DXT"]),
                               (s["TT"], mk["mmTT"])] if c == 0 else []),
                       incs=([(s["SC"], 1)] if c == 7 else []))
                mk["evSC"] = mark(s["SC"])
                OP("gpsimd", lambda q, i128=i128: q.dma_start(
                    out=xtfmd[:, i128], in_=xtout.rearrange("p (c t) -> p c t", t=128)),
                   waits=[(s["SC"], mk["evSC"])], incs=[(s["DST"], 16)])
                mk["DXT"] = mark(s["DST"])
                for c in range(8):
                    OP("tensor", lambda q, c=c: q.matmul(
                        pG, xtout[:, c * 128:(c + 1) * 128], gW[:, c * 8:(c + 1) * 8],
                        start=(c == 0), stop=(c == 7)),
                       waits=([(s["SC"], mk["evSC"]), (s["VC"], mk["MSK"])] if c == 0 else []))
                for c in range(8):
                    OP("tensor", lambda q, c=c: q.matmul(
                        pt1, xtout[:, c * 128:(c + 1) * 128], a1W[:, c * 384:(c + 1) * 384],
                        start=(c == 0), stop=(c == 7)))
                for c in range(8):
                    OP("tensor", lambda q, c=c: q.matmul(
                        pt3, xtout[:, c * 128:(c + 1) * 128], a3W[:, c * 384:(c + 1) * 384],
                        start=(c == 0), stop=(c == 7)),
                       incs=([(s["TT"], 1)] if c == 7 else []))
                mk["mmTT"] = mark(s["TT"])
                OP("vector", lambda q: q.max(out=top8, in_=pG),
                   waits=[(s["TT"], mk["mmTT"])], incs=[(s["VC"], 1)])
                m_m8 = mark(s["VC"])
                OP("scalar", lambda q: q.copy(top8b, top8), waits=[(s["VC"], m_m8)])
                OP("scalar", lambda q: q.mul(negv1, top8b[:, 0:1], -1.0))
                OP("scalar", lambda q: q.copy(t1s, pt1))
                OP("scalar", lambda q: q.copy(t3s, pt3), incs=[(s["SC"], 1)])
                m_tev = mark(s["SC"])
                OP("scalar", lambda q: q.activation(e8, pG, AF.Exp, bias=negv1))
                OP("scalar", lambda q: q.activation(rdenb[:, 0:1], top8b[:, 1:2], AF.Sigmoid,
                                                    scale=-1.0, bias=top8b[:, 0:1]),
                   incs=[(s["SC"], 1)])
                m_rdb = mark(s["SC"])
                OP("vector", lambda q: q.tensor_scalar(msk, pG, top8b[:, 1:2], None, ALU.is_ge),
                   waits=[(s["SC"], m_rdb)], incs=[(s["VC"], 1)])
                mk["MSK"] = mark(s["VC"])
                OP("vector", lambda q: q.scalar_tensor_tensor(
                    w_tm, e8, rdenb[:, 0:1], msk, ALU.mult, ALU.mult))
                OP("vector", lambda q: q.tensor_copy(den, w_tm))
                for e in range(8):
                    OP("vector", lambda q, e=e: q.tensor_copy(
                        wpad[:, 32 * e:32 * e + 1], w_tm[:, e:e + 1]),
                       waits=([(s["TT"], mk["WT"])] if e == 0 else []),
                       incs=([(s["VC"], 1)] if e == 7 else []))
                m_wp = mark(s["VC"])
                for j2 in range(2):
                    OP("tensor", lambda q, j2=j2: q.transpose(
                        pW[:, j2 * 128:(j2 + 1) * 128], wpad[:, j2 * 128:(j2 + 1) * 128],
                        idsb[:]),
                       waits=([(s["VC"], m_wp), (s["DST"], mk["WEV"]),
                               (s["SC"], mk["T2"])] if j2 == 0 else []),
                       incs=([(s["TT"], 1)] if j2 == 1 else []))
                m_wt = mark(s["TT"])
                mk["WT"] = m_wt
                OP("scalar", lambda q: q.copy(wfm, pW[:, 0:256]), waits=[(s["TT"], m_wt)],
                   incs=[(s["SC"], 1)])
                m_wf = mark(s["SC"])
                for e in range(8):
                    j2, i4 = divmod(e, 4)
                    OP("sync", lambda q, e=e, j2=j2, i4=i4, i128=i128: q.dma_start(
                        out=wgtd[e:e + 1, i128 * 128:(i128 + 1) * 128],
                        in_=wfm[32 * i4:32 * i4 + 1, j2 * 128:(j2 + 1) * 128].bitcast(F32)),
                       waits=([(s["SC"], m_wf)] if e == 0 else []), incs=[(s["DST"], 16)])
                for g in range(3):
                    OP("tensor", lambda q, g=g: q.transpose(
                        pt1[:, g * 128:(g + 1) * 128],
                        t1s[:, g * 128:(g + 1) * 128], idsb[:]),
                       waits=([(s["SC"], m_tev), (s["SC"], m_wf)] if g == 0 else []))
                for g in range(3):
                    OP("tensor", lambda q, g=g: q.transpose(
                        pt3[:, g * 128:(g + 1) * 128],
                        t3s[:, g * 128:(g + 1) * 128], idsb[:]),
                       incs=([(s["TT"], 1)] if g == 2 else []))
                m_t2 = mark(s["TT"])
                OP("scalar", lambda q: q.copy(t1out, pt1[:, 0:384]),
                   waits=[(s["TT"], m_t2), (s["DST"], mk["WEV"])])
                OP("scalar", lambda q: q.copy(t3out, pt3[:, 0:384]), incs=[(s["SC"], 1)])
                mk["T2"] = mark(s["SC"])
                if self.dev and i128 == 0:
                    OP("gpsimd", lambda q: q.dma_start(out=self.D["dbg"][:, 0:8], in_=top8),
                       waits=[(s["SC"], mk["T2"])], incs=[(s["DST"], 16)])
                    OP("gpsimd", lambda q: q.dma_start(out=self.D["dbg"][:, 8:16], in_=e8),
                       incs=[(s["DST"], 16)])
                    OP("gpsimd", lambda q: q.dma_start(out=self.D["dbg"][:, 16:24], in_=msk),
                       incs=[(s["DST"], 16)])
                    OP("gpsimd", lambda q: q.dma_start(out=self.D["dbg"][:, 24:32], in_=w_tm),
                       incs=[(s["DST"], 16)])
                    OP("gpsimd", lambda q: q.dma_start(out=self.D["dbg"][:, 32:288],
                                                       in_=wfm.bitcast(F32)),
                       incs=[(s["DST"], 16)])
                    OP("gpsimd", lambda q: q.dma_start(out=self.D["dbg"][:, 288:544],
                                                       in_=wpad),
                       incs=[(s["DST"], 16)])
                    OP("gpsimd", lambda q: q.dma_start(out=self.D["dbg"][:, 544:552],
                                                       in_=top8b),
                       incs=[(s["DST"], 16)])
                    OP("gpsimd", lambda q: q.dma_start(out=self.D["dbg"][:, 552:553],
                                                       in_=negv1, allow_slow_non_contiguous=True),
                       incs=[(s["DST"], 16)])
                    OP("gpsimd", lambda q: q.dma_start(out=self.D["dbg"][:, 553:554],
                                                       in_=rdenb[:, 0:1], allow_slow_non_contiguous=True),
                       incs=[(s["DST"], 16)])
                    OP("gpsimd", lambda q: q.dma_start(out=self.D["dbg"][:, 554:555],
                                                       in_=ev2, allow_slow_non_contiguous=True),
                       incs=[(s["DST"], 16)])
                    OP("gpsimd", lambda q: q.dma_start(out=self.D["dbg"][:, 555:556],
                                                       in_=den[:, 0:1], allow_slow_non_contiguous=True),
                       incs=[(s["DST"], 16)])
                    OP("gpsimd", lambda q: q.dma_start(out=self.D["dbg"][:, 557:565],
                                                       in_=wnum),
                       incs=[(s["DST"], 16)])
                OP("gpsimd", lambda q, i128=i128: q.dma_start(
                    out=t1pd[:, i128], in_=t1out.rearrange("p (j t) -> p j t", t=128)),
                   waits=[(s["SC"], mk["T2"])], incs=[(s["DST"], 16)])
                OP("gpsimd", lambda q, i128=i128: q.dma_start(
                    out=t3pd[:, i128], in_=t3out.rearrange("p (j t) -> p j t", t=128)),
                   incs=[(s["DST"], 16)])
                mk["WEV"] = mark(s["DST"])

        def phase_F(fc):
            bar = [(nm_, P.v(nm_)) for nm_ in s.values()]
            F0 = fc * 512
            wgT = WA[:, 0:4096]
            wuT = WA[:, 4096:8192]
            wdT = WA[:, 8192:12288]
            B1c = WA[:, 12288:13824]
            B3c = WA[:, 13824:15360]
            A2c = WA[:, 15360:15872]
            t1p_s = WA[:, 15872:17408]
            t3p_s = WA[:, 17408:18944]
            act2 = [WA[:, 18944:19456], WA[:, 19456:19968]]
            xtfm_s = SMM[:, 0:4096]
            mix = SMM[:, 4096:6144]
            se2 = [SMM[:, 6144:6656], SMM[:, 6656:7168]]
            be2 = [SMM[:, 7168:7680], SMM[:, 7680:8192]]
            wb = SMM[:, 8192:12288]
            u2 = [SF32[:, 0:512], SF32[:, 512:1024]]
            stage_o = [SF32[:, 1024:2048], SF32[:, 2048:3072]]
            stz = SF32[:, 3072:3584]
            zscr = SF32[:, 3584:4096]
            cw1 = SF32[:, 4096:4608]
            cw3 = SF32[:, 4608:5120]
            pcw1 = ps[:, 0:512]
            pcw3 = ps[:, 512:1024]
            pd1 = lambda ei: ps[:, 1024 + (ei % 2) * 1024: 1024 + (ei % 2) * 1024 + 512]
            pd3 = lambda ei: ps[:, 1536 + (ei % 2) * 1024: 1536 + (ei % 2) * 1024 + 512]
            pz2 = lambda mc: ps[:, 3072 + mc * 128: 3072 + (mc + 1) * 128]
            pO = ps[:, 3584:4096]

            for nm_, dstw, srcw, n_ in [
                ("wgT", wgT, W["w_gateT"][:, F0:F0 + 512], 512),
                ("wuT", wuT, W["w_upT"][:, F0:F0 + 512], 512),
            ]:
                OP("sync", lambda q, dstw=dstw, srcw=srcw, n_=n_: q.dma_start(
                    out=dstw.rearrange("p (c n) -> p c n", n=n_),
                    in_=srcw.rearrange("(c p) n -> p c n", p=128)),
                   waits=(bar if nm_ == "wgT" else []), incs=[(s["DLW"], 16)])
            OP("sync", lambda q: q.dma_start(
                out=wdT.rearrange("p (k n) -> p k n", n=1024),
                in_=W["w_downT"][F0:F0 + 512, :].rearrange("(k p) n -> p k n", p=128)),
               incs=[(s["DLW"], 16)])
            OP("sync", lambda q: q.dma_start(
                out=B1c.rearrange("p (g n) -> p g n", n=512),
                in_=W["B1cat"][:, :, F0:F0 + 512].rearrange("g p n -> p g n")),
               incs=[(s["DLW"], 16)])
            OP("sync", lambda q: q.dma_start(
                out=B3c.rearrange("p (g n) -> p g n", n=512),
                in_=W["B3cat"][:, :, F0:F0 + 512].rearrange("g p n -> p g n")),
               incs=[(s["DLW"], 16)])
            OP("sync", lambda q: q.dma_start(
                out=A2c.rearrange("p (c n) -> p c n", n=128),
                in_=W["A2T"][F0:F0 + 512, :].rearrange("(c p) n -> p c n", p=128)),
               incs=[(s["DLW"], 16)])
            m_w = mark(s["DLW"])
            first = {k: [bar] for k in ("vector", "scalar", "gpsimd", "tensor")}

            def fw(k):
                return first[k].pop() if first[k] else []

            mk = dict(eDone=0, abV={}, aSC={}, mixV={}, zTT={}, uGP={}, actGP={},
                      OdD=[0, 0], stzD=0, wbGP=0, cwSC={})

            for j in range(NTOK // 512):
                for c_ in range(8):
                    OP("sync", lambda q, j=j, c_=c_: q.dma_start(
                        out=xtfm_s[:, c_ * 512:(c_ + 1) * 512].rearrange(
                            "p (t k) -> p t k", k=128),
                        in_=xtfmd[:, 4 * j:4 * j + 4, c_, :]),
                       waits=([(s["TT"], mk["eDone"])] if c_ == 0 else []),
                       incs=[(s["DLX"], 16)])
                for g in range(3):
                    OP("sync", lambda q, j=j, g=g: q.dma_start(
                        out=t1p_s[:, g * 512:(g + 1) * 512].rearrange(
                            "p (t k) -> p t k", k=128),
                        in_=t1pd[:, 4 * j:4 * j + 4, g, :]), incs=[(s["DLX"], 16)])
                    OP("sync", lambda q, j=j, g=g: q.dma_start(
                        out=t3p_s[:, g * 512:(g + 1) * 512].rearrange(
                            "p (t k) -> p t k", k=128),
                        in_=t3pd[:, 4 * j:4 * j + 4, g, :]), incs=[(s["DLX"], 16)])
                for e in range(8):
                    OP("gpsimd", lambda q, e=e, j=j: q.dma_start(
                        out=wb[:, e * 512:(e + 1) * 512],
                        in_=wgtd[e:e + 1, j * 512:(j + 1) * 512].to_broadcast([128, 512])),
                       waits=([(s["GP"], mk["wbGP"])] if e == 0 else []),
                       incs=[(s["DLX"], 16)])
                m_ld = mark(s["DLX"])
                for fsub in range(4):
                    for c_ in range(8):
                        OP("tensor", lambda q, c_=c_, fsub=fsub: q.matmul(
                            pcw1, wgT[:, c_ * 512 + fsub * 128: c_ * 512 + (fsub + 1) * 128],
                            xtfm_s[:, c_ * 512:(c_ + 1) * 512],
                            start=(c_ == 0), stop=(c_ == 7)),
                           waits=(([(s["DLX"], m_ld), (s["DLW"], m_w),
                                    (s["SC"], mk["cwSC"].get(fsub - 1, 0))]
                                   + fw("tensor")) if c_ == 0 else []))
                    for c_ in range(8):
                        OP("tensor", lambda q, c_=c_, fsub=fsub: q.matmul(
                            pcw3, wuT[:, c_ * 512 + fsub * 128: c_ * 512 + (fsub + 1) * 128],
                            xtfm_s[:, c_ * 512:(c_ + 1) * 512],
                            start=(c_ == 0), stop=(c_ == 7)),
                           incs=([(s["TT"], 1)] if c_ == 7 else []))
                    m_cw = mark(s["TT"])
                    OP("scalar", lambda q: q.copy(cw1, pcw1),
                       waits=[(s["TT"], m_cw),
                              (s["VC"], mk["abV"].get((fsub - 1, 7), 0))] + fw("scalar"))
                    OP("scalar", lambda q: q.copy(cw3, pcw3), incs=[(s["SC"], 1)])
                    mk["cwSC"][fsub] = mark(s["SC"])
                    m_cwe = mk["cwSC"][fsub]
                    for e in range(8):
                        ei = fsub * 8 + e
                        g, pos = divmod(e, 3)
                        prev2 = (fsub, e - 2) if e >= 2 else (fsub - 1, e + 6)
                        OP("tensor", lambda q, g=g, pos=pos, fsub=fsub, ei=ei: q.matmul(
                            pd1(ei),
                            B1c[32 * pos:32 * pos + 16,
                                g * 512 + fsub * 128: g * 512 + (fsub + 1) * 128],
                            t1p_s[32 * pos:32 * pos + 16, g * 512:(g + 1) * 512],
                            start=True, stop=True),
                           waits=[(s["VC"], mk["abV"].get(prev2, 0)),
                                  (s["SC"], m_cwe if e == 0 else 0)])
                        OP("tensor", lambda q, g=g, pos=pos, fsub=fsub, ei=ei: q.matmul(
                            pd3(ei),
                            B3c[32 * pos:32 * pos + 16,
                                g * 512 + fsub * 128: g * 512 + (fsub + 1) * 128],
                            t3p_s[32 * pos:32 * pos + 16, g * 512:(g + 1) * 512],
                            start=True, stop=True), incs=[(s["TT"], 1)])
                        m_d = mark(s["TT"])
                        mk["zTT"][(fsub, e)] = 0
                        OP("vector", lambda q, ei=ei: q.scalar_tensor_tensor(
                            se2[ei % 2], pd1(ei), 2.0, cw1, ALU.mult, ALU.add),
                           waits=[(s["TT"], m_d), (s["SC"], mk["aSC"].get(prev2, 0)),
                                  (s["GP"], mk["actGP"].get(prev2, 0))] + fw("vector"))
                        OP("vector", lambda q, ei=ei: q.scalar_tensor_tensor(
                            be2[ei % 2], pd3(ei), 2.0, cw3, ALU.mult, ALU.add),
                           incs=[(s["VC"], 1)])
                        mk["abV"][(fsub, e)] = mark(s["VC"])
                        OP("scalar", lambda q, ei=ei: q.activation(
                            se2[ei % 2], se2[ei % 2], AF.Silu),
                           waits=[(s["VC"], mk["abV"][(fsub, e)])], incs=[(s["SC"], 1)])
                        mk["aSC"][(fsub, e)] = mark(s["SC"])
                        OP("gpsimd", lambda q, ei=ei: q.tensor_tensor(
                            act2[ei % 2], se2[ei % 2], be2[ei % 2], ALU.mult),
                           waits=[(s["SC"], mk["aSC"][(fsub, e)]),
                                  (s["TT"], mk["zTT"].get(prev2, 0))] + fw("gpsimd"),
                           incs=[(s["GP"], 1)])
                        mk["actGP"][(fsub, e)] = mark(s["GP"])
                        m_act = mk["actGP"][(fsub, e)]
                        for mc in range(4):
                            OP("tensor", lambda q, e=e, mc=mc, ei=ei: q.matmul(
                                pz2(mc)[:, 16 * e:16 * e + 16],
                                act2[ei % 2][:, mc * 128:(mc + 1) * 128],
                                A2c[:, (ei // 8) * 128 + 16 * e:
                                    (ei // 8) * 128 + 16 * e + 16],
                                start=(ei == 0 and mc == 0),
                                stop=(fsub == 3 and e == 7),
                                skip_group_check=True),
                               waits=([(s["GP"], m_act)] if mc == 0 else []),
                               incs=([(s["TT"], 1)] if mc == 3 else []))
                        mk["zTT"][(fsub, e)] = mark(s["TT"])
                        OP("gpsimd", lambda q, e=e, ei=ei: q.tensor_tensor(
                            u2[ei % 2], act2[ei % 2], wb[:, e * 512:(e + 1) * 512],
                            ALU.mult),
                           waits=[(s["DLX"], m_ld),
                                  (s["VC"], mk["mixV"].get(prev2, 0))],
                           incs=[(s["GP"], 1)])
                        mk["uGP"][(fsub, e)] = mark(s["GP"])
                        if e == 0:
                            OP("vector", lambda q, fsub=fsub, ei=ei: q.tensor_copy(
                                mix[:, fsub * 512:(fsub + 1) * 512], u2[ei % 2]),
                               waits=[(s["GP"], mk["uGP"][(fsub, e)]),
                                      (s["TT"], mk["eDone"])],
                               incs=[(s["VC"], 1)])
                        else:
                            OP("vector", lambda q, fsub=fsub, ei=ei: q.tensor_tensor(
                                mix[:, fsub * 512:(fsub + 1) * 512],
                                mix[:, fsub * 512:(fsub + 1) * 512], u2[ei % 2],
                                ALU.add),
                               waits=[(s["GP"], mk["uGP"][(fsub, e)])], incs=[(s["VC"], 1)])
                        mk["mixV"][(fsub, e)] = mark(s["VC"])
                m_mix = mark(s["VC"])
                mk["wbGP"] = mark(s["GP"])
                prev_stage = 0
                for m_ in range(4):
                    for n in range(2):
                        for k_ in range(4):
                            OP("tensor", lambda q, m_=m_, n=n, k_=k_: q.matmul(
                                pO,
                                mix[:, k_ * 512 + m_ * 128: k_ * 512 + (m_ + 1) * 128],
                                wdT[:, k_ * 1024 + n * 512: k_ * 1024 + (n + 1) * 512],
                                start=(k_ == 0), stop=(k_ == 3)),
                               waits=([(s["VC"], m_mix), (s["SC"], prev_stage)]
                                      if k_ == 0 else []),
                               incs=([(s["TT"], 1)] if k_ == 3 else []))
                        m_o = mark(s["TT"])
                        OP("scalar", lambda q, st=stage_o[m_ % 2], n=n: q.copy(
                            st[:, n * 512:(n + 1) * 512], pO),
                           waits=[(s["TT"], m_o)] +
                                 ([(s["DST"], mk["OdD"][m_ % 2])] if n == 0 else []),
                           incs=[(s["SC"], 1)])
                        prev_stage = mark(s["SC"])
                    i128 = 4 * j + m_
                    OP("gpsimd", lambda q, st=stage_o[m_ % 2], i128=i128: q.dma_start(
                        out=outd[i128 * 128:(i128 + 1) * 128, :], in_=st,
                        accum_op=(ALU.add if fc > 0 else ALU.bypass)),
                       waits=[(s["SC"], prev_stage)], incs=[(s["DST"], 16)])
                    mk["OdD"][m_ % 2] = mark(s["DST"])
                for mc in range(4):
                    OP("scalar", lambda q, mc=mc: q.copy(
                        zscr[:, mc * 128:(mc + 1) * 128], pz2(mc)),
                       waits=([(s["TT"], mk["zTT"][(3, 7)]), (s["DST"], mk["stzD"])]
                              if mc == 0 else []),
                       incs=([(s["SC"], 1)] if mc == 3 else []))
                m_ze = mark(s["SC"])
                for mc in range(4):
                    OP("tensor", lambda q, mc=mc: q.transpose(
                        ps[:, 1024 + mc * 128: 1024 + (mc + 1) * 128],
                        zscr[:, mc * 128:(mc + 1) * 128], idsb[:]),
                       waits=([(s["SC"], m_ze),
                               (s["VC"], mk["abV"][(3, 7)])] if mc == 0 else []),
                       incs=([(s["TT"], 1)] if mc == 3 else []))
                m_ztr = mark(s["TT"])
                OP("scalar", lambda q: q.copy(stz, ps[:, 1024:1536]),
                   waits=[(s["TT"], m_ztr)], incs=[(s["SC"], 1)])
                m_ze2 = mark(s["SC"])
                OP("gpsimd", lambda q, j=j: q.dma_start(
                    out=zd[:, j * 512:(j + 1) * 512], in_=stz,
                    accum_op=(ALU.add if fc > 0 else ALU.bypass)),
                   waits=[(s["SC"], m_ze2)], incs=[(s["DST"], 16)])
                mk["stzD"] = mark(s["DST"])
                mk["eDone"] = mark(s["TT"])

        def phase_G():
            bar = [(nm_, P.v(nm_)) for nm_ in s.values()]
            b2W = WA[:, 0:1024]
            z_s = SMM[:, 0:512]
            z_s2 = SMM[:, 512:1024]
            xm = [SF32[:, 0:1024], SF32[:, 1024:2048]]
            om = [SF32[:, 2048:3072], SF32[:, 3072:4096]]
            tsum = SF32[:, 4096:5120]
            wbz = SF32[:, 5120:5632]
            pO = ps[:, 0:2048]
            OP("sync", lambda q: q.dma_start(out=b2W, in_=W["B2catT"]),
               waits=bar, incs=[(s["DLW"], 16)])
            m_w = mark(s["DLW"])
            mk = dict(RSTD=0, zTT=0, yD=[0, 0], vD=[0, 0], ZS=0, xSC=[0, 0])
            first = {k: [bar] for k in ("vector", "scalar", "gpsimd", "tensor")}

            def fw(k):
                return first[k].pop() if first[k] else []

            for j in range(NTOK // 512):
                OP("sync", lambda q, j=j: q.dma_start(
                    out=z_s, in_=zd[:, j * 512:(j + 1) * 512]),
                   waits=[(s["TT"], mk["zTT"])] + (bar if j == 0 else []),
                   incs=[(s["DLX"], 16)])
                for e in range(8):
                    OP("sync", lambda q, e=e, j=j: q.dma_start(
                        out=wbz[16 * e:16 * e + 16, :],
                        in_=wgtd[e:e + 1, j * 512:(j + 1) * 512].to_broadcast([16, 512])),
                       waits=([(s["VC"], mk["ZS"])] if e == 0 else []),
                       incs=[(s["DLX"], 16)])
                m_z = mark(s["DLX"])
                OP("vector", lambda q: q.tensor_tensor(z_s2, z_s, wbz, ALU.mult),
                   waits=[(s["DLX"], m_z), (s["TT"], mk["zTT"])] + fw("vector"),
                   incs=[(s["VC"], 1)])
                m_zs = mark(s["VC"])
                mk["ZS"] = m_zs
                for m_ in range(4):
                    i128 = 4 * j + m_
                    OP("sync", lambda q, i128=i128, m_=m_: q.dma_start(
                        out=xm[m_ % 2], in_=xtd[i128 * 128:(i128 + 1) * 128, :]),
                       waits=[(s["VC"], mk["vD"][m_ % 2]), (s["SC"], mk["xSC"][m_ % 2])],
                       incs=[(s["DLX"], 16)])
                    OP("sync", lambda q, i128=i128, m_=m_: q.dma_start(
                        out=om[m_ % 2], in_=outd[i128 * 128:(i128 + 1) * 128, :]),
                       waits=[(s["DST"], mk["yD"][m_ % 2])], incs=[(s["DLX"], 16)])
                    m_x = mark(s["DLX"])
                    for n in range(2):
                        OP("tensor", lambda q, n=n, m_=m_: q.matmul(
                            pO[:, (m_ % 2) * 1024 + n * 512:
                               (m_ % 2) * 1024 + (n + 1) * 512],
                            z_s2[:, m_ * 128:(m_ + 1) * 128],
                            b2W[:, n * 512:(n + 1) * 512],
                            start=True, stop=True),
                           waits=(([(s["VC"], m_zs), (s["DLW"], m_w),
                                    (s["VC"], mk["vD"][m_ % 2])] + fw("tensor"))
                                  if n == 0 else []),
                           incs=([(s["TT"], 1)] if n == 1 else []))
                    m_mm = mark(s["TT"])
                    if m_ == 3:
                        mk["zTT"] = m_mm
                    OP("vector", lambda q, m_=m_: q.scalar_tensor_tensor(
                        tsum, pO[:, (m_ % 2) * 1024:(m_ % 2) * 1024 + 1024], 2.0,
                        om[m_ % 2], ALU.mult, ALU.add),
                       waits=[(s["TT"], m_mm), (s["DLX"], m_x), (s["SC"], mk["RSTD"])]
                             + fw("vector"))
                    OP("vector", lambda q, m_=m_: q.tensor_tensor(
                        xm[m_ % 2], tsum, xm[m_ % 2], ALU.add), incs=[(s["VC"], 1)])
                    OP("vector", lambda q, m_=m_: q.reduce_sum(
                        st1[:], xm[m_ % 2], axis=AX.X, negate=True), incs=[(s["VC"], 1)])
                    m_sum = mark(s["VC"])
                    OP("scalar", lambda q: q.mul(nm1[:], st1[:], 1.0 / 1024),
                       waits=[(s["VC"], m_sum)] + fw("scalar"))
                    OP("scalar", lambda q, m_=m_: q.activation(
                        tsum, xm[m_ % 2], AF.Square, bias=st1[:], scale=1024.0,
                        accum_out=sq1[:]))
                    OP("scalar", lambda q: q.activation(lv1[:], sq1[:], AF.Ln, bias=eps_sb[:],
                                                        scale=1.0 / (1024.0 ** 3)))
                    OP("scalar", lambda q: q.activation(rstd1[:], lv1[:], AF.Exp, scale=-0.5),
                       incs=[(s["SC"], 1)])
                    mk["RSTD"] = mark(s["SC"])
                    mk["xSC"][m_ % 2] = mk["RSTD"]
                    OP("vector", lambda q, m_=m_: q.tensor_scalar(
                        om[m_ % 2], xm[m_ % 2], nm1[:], rstd1[:], ALU.add, ALU.mult),
                       waits=[(s["SC"], mk["RSTD"]), (s["DST"], mk["yD"][m_ % 2])],
                       incs=[(s["VC"], 1)])
                    mk["vD"][m_ % 2] = mark(s["VC"])
                    OP("gpsimd", lambda q, i128=i128, m_=m_: q.dma_start(
                        out=y_out[i128 * 128:(i128 + 1) * 128, :], in_=om[m_ % 2]),
                       waits=[(s["VC"], mk["vD"][m_ % 2])] + fw("gpsimd"),
                       incs=[(s["DST"], 16)])
                    mk["yD"][m_ % 2] = mark(s["DST"])

        self.phase_T = phase_T
        self.phase_F = phase_F
        self.phase_G = phase_G

        phases = self.cfg.get("phases", ["A"])
        for l in range(nl):
            if "A" in phases:
                phase_A(l)
            if "B" in phases:
                phase_B(l)
            if "C" in phases:
                phase_C(l, 0)
                phase_C(l, 2048)
        if "T" in phases:
            phase_T()
        if "F" in phases:
            for fc in range(8):
                phase_F(fc)
        if "G" in phases:
            phase_G()

        OP("gpsimd", lambda q: q.nop(),
           waits=[(s["DST"], P.v(s["DST"])), (s["DLX"], P.v(s["DLX"])),
                  (s["DLW"], P.v(s["DLW"]))])

        block = self.ctx.enter_context(nc.Block())
        P.emit(block, self.handles)
        self.ctx.close()
        return nc


from concourse.bass_utils import run_bass_kernel_spmd


def kernel(**inputs):
    in_maps = shard_inputs(inputs, 8)
    m = MK(dict(f32r=True, dev=False, nlayers=6, T=4096, nseq=2,
                phases=["A", "B", "C", "T", "F", "G"]))
    nc = m.build()
    res = run_bass_kernel_spmd(nc, in_maps, core_ids=list(range(8)))
    return np.concatenate([r["y_out"].reshape(2, 4096, 1024) for r in res.results],
                          axis=0)


# version 05
# speedup vs baseline: 1.0031x; 1.0031x over previous
import numpy as np
import concourse.bass as bass
import concourse.mybir as mybir
from contextlib import ExitStack

F32 = mybir.dt.float32
F32R = mybir.dt.float32r
AF = mybir.ActivationFunctionType
ALU = mybir.AluOpType
AX = mybir.AxisListType

ENGS = ("sync", "tensor", "vector", "scalar", "gpsimd")


class Plan:

    def __init__(self):
        self.cnt = {}
        self.items = []
        self.sem_names = []

    def sem(self, name):
        if name not in self.cnt:
            self.cnt[name] = 0
            self.sem_names.append(name)
        return name

    def v(self, s):
        return self.cnt[s]

    def op(self, eng, fn, waits=(), incs=()):
        for s, n in incs:
            self.cnt[s] = self.cnt[s] + n
        self.items.append((eng, fn, tuple(waits), tuple(incs)))

    def emit(self, block, handles):
        for engname in ENGS:
            items = [(fn, w, i) for (e, fn, w, i) in self.items if e == engname]
            if not items:
                continue

            def section(q, _items=items, _eng=engname):
                guaranteed = {}
                for fn, waits, incs in _items:
                    for s, v in waits:
                        if guaranteed.get(s, 0) < v:
                            q.wait_ge(handles[s], v)
                            guaranteed[s] = v
                    ins = fn(q)
                    for s, n in incs:
                        ins.then_inc(handles[s], n)
                        guaranteed[s] = guaranteed.get(s, 0) + n if False else guaranteed.get(s, 0)

            getattr(block, engname)(section)


def shard_inputs(inputs, n_cores=8):
    d = inputs
    DIM, F, E, R, DEPTH = 1024, 4096, 8, 16, 6
    x = np.ascontiguousarray(np.asarray(d["L0_attn_x"], dtype=np.float32))
    B, T, _ = x.shape
    per = B // n_cores

    def f32c(a):
        return np.ascontiguousarray(np.asarray(a, dtype=np.float32))

    shared = {}
    shared["ident"] = np.eye(128, dtype=np.float32)
    shared["ones1"] = np.ones((1, 128), dtype=np.float32)
    shared["onescol"] = np.ones((128, 2), dtype=np.float32)
    for nm in ["wq", "wk", "wv", "wo", "w1", "w2"]:
        shared[nm] = f32c(d["L0_attn_" + nm])
    lncols = np.zeros((128, 8 * 4 * DEPTH), dtype=np.float32)
    for li in range(DEPTH):
        for ki, nm in enumerate(["ln1_w", "ln1_b", "ln2_w", "ln2_b"]):
            vcol = f32c(d["L0_attn_" + nm])[li]
            lncols[:, (li * 4 + ki) * 8 : (li * 4 + ki + 1) * 8] = vcol.reshape(8, 128).T
    shared["lncols"] = lncols
    shared["bo"] = f32c(d["L0_attn_bo"])
    shared["gate_wT"] = f32c(d["L0_ffn_gate_w"]).T.copy()
    shared["w_gateT"] = f32c(d["L0_ffn_w_gate"]).T.copy()
    shared["w_upT"] = f32c(d["L0_ffn_w_up"]).T.copy()
    shared["w_downT"] = f32c(d["L0_ffn_w_down"]).T.copy()
    A1, A3 = f32c(d["L0_ffn_A1"]), f32c(d["L0_ffn_A3"])
    B1, B3 = f32c(d["L0_ffn_B1"]), f32c(d["L0_ffn_B3"])
    A2, B2 = f32c(d["L0_ffn_A2"]), f32c(d["L0_ffn_B2"])
    a1cat = np.zeros((DIM, 384), dtype=np.float32)
    a3cat = np.zeros((DIM, 384), dtype=np.float32)
    for e in range(E):
        g, pos = divmod(e, 3)
        a1cat[:, 128 * g + 32 * pos : 128 * g + 32 * pos + 16] = A1[e].T
        a3cat[:, 128 * g + 32 * pos : 128 * g + 32 * pos + 16] = A3[e].T
    shared["A1cat"] = a1cat
    shared["A3cat"] = a3cat
    b1cat = np.zeros((3, 128, F), dtype=np.float32)
    b3cat = np.zeros((3, 128, F), dtype=np.float32)
    b2catT = np.zeros((128, DIM), dtype=np.float32)
    for e in range(E):
        g, pos = divmod(e, 3)
        b1cat[g, 32 * pos : 32 * pos + 16, :] = B1[e].T
        b3cat[g, 32 * pos : 32 * pos + 16, :] = B3[e].T
        b2catT[16 * e : 16 * e + 16, :] = B2[e].T
    shared["B1cat"] = b1cat
    shared["B3cat"] = b3cat
    shared["B2catT"] = b2catT
    a2T = np.zeros((F, 128), dtype=np.float32)
    for e in range(E):
        a2T[:, 16 * e : 16 * e + 16] = A2[e].T
    shared["A2T"] = a2T

    in_maps = []
    for c in range(n_cores):
        m = dict(shared)
        m["x_in"] = x[c * per : (c + 1) * per].reshape(per * T, DIM)
        in_maps.append(m)
    return in_maps


class MK:
    def __init__(self, cfg):
        self.cfg = cfg
        self.nc = bass.Bass("TRN2", target_bir_lowering=False)
        self.plan = Plan()
        self.ctx = ExitStack()
        self.handles = {}
        self.mmdt = F32R if cfg.get("f32r", True) else F32
        self.dev = cfg.get("dev", False)
        self.nlayers = cfg.get("nlayers", 6)
        self.T = cfg.get("T", 4096)
        self.nseq = cfg.get("nseq", 2)
        self.NTOK = self.T * self.nseq
        self.ntiles = self.NTOK // 128
        self.nt512 = self.NTOK // 512
        self.with_ffn = cfg.get("ffn", True)

    def sem(self, name):
        self.handles[name] = self.ctx.enter_context(self.nc.semaphore("s_" + name))
        return self.plan.sem(name)

    def sb(self, name, shape, dtype=F32):
        return self.ctx.enter_context(self.nc.sbuf_tensor(name, shape, dtype))

    def dram(self, name, shape, dtype=F32, kind="Internal"):
        if kind == "Internal" and self.dev:
            kind = "ExternalOutput"
        return self.nc.dram_tensor(name, shape, dtype, kind=kind).ap()

    def inp(self, name, shape, dtype=F32):
        return self.nc.dram_tensor(name, shape, dtype, kind="ExternalInput").ap()

    def build(self):
        nc, P = self.nc, self.plan
        mmdt = self.mmdt
        NTOK, ntiles, T = self.NTOK, self.ntiles, self.T
        tiles_per_seq = T // 128
        nl = self.nlayers

        x_in = self.inp("x_in", [NTOK, 1024])
        ident = self.inp("ident", [128, 128])
        ones1 = self.inp("ones1", [1, 128], mmdt)
        lncols = self.inp("lncols", [128, 8 * 4 * 6])
        W = {}
        for nm in ["wq", "wk", "wv", "wo"]:
            W[nm] = self.inp(nm, [6, 1024, 1024], mmdt)
        W["w1"] = self.inp("w1", [6, 1024, 4096], mmdt)
        W["w2"] = self.inp("w2", [6, 4096, 1024], mmdt)
        W["bo"] = self.inp("bo", [6, 1024], F32)
        W["onescol"] = self.inp("onescol", [128, 2], mmdt)
        W["gate_wT"] = self.inp("gate_wT", [1024, 8], mmdt)
        W["w_gateT"] = self.inp("w_gateT", [1024, 4096], mmdt)
        W["w_upT"] = self.inp("w_upT", [1024, 4096], mmdt)
        W["w_downT"] = self.inp("w_downT", [4096, 1024], mmdt)
        W["A1cat"] = self.inp("A1cat", [1024, 384], mmdt)
        W["A3cat"] = self.inp("A3cat", [1024, 384], mmdt)
        W["B1cat"] = self.inp("B1cat", [3, 128, 4096], mmdt)
        W["B3cat"] = self.inp("B3cat", [3, 128, 4096], mmdt)
        W["B2catT"] = self.inp("B2catT", [128, 1024], mmdt)
        W["A2T"] = self.inp("A2T", [4096, 128], mmdt)
        self.W = W

        y_out = self.dram("y_out", [NTOK, 1024], F32, kind="ExternalOutput")
        resid = self.dram("resid", [NTOK, 1024], F32)
        qsd = self.dram("qsd", [128, ntiles, 8, 128], mmdt)
        xn2fmd = self.dram("xn2fmd", [128, ntiles, 8, 128], mmdt)
        xtd = self.dram("xtd", [NTOK, 1024], F32)
        xtfmd = self.dram("xtfmd", [128, ntiles, 8, 128], mmdt)
        t1pd = self.dram("t1pd", [128, ntiles, 3, 128], mmdt)
        t3pd = self.dram("t3pd", [128, ntiles, 3, 128], mmdt)
        wgtd = self.dram("wgtd", [8, NTOK], F32)
        zd = self.dram("zd0", [128, NTOK], mmdt)
        outd = self.dram("outd", [NTOK, 1024], F32)
        self.D = dict(resid=resid, qsd=qsd, xn2fmd=xn2fmd, xtd=xtd, xtfmd=xtfmd,
                      t1pd=t1pd, t3pd=t3pd, wgtd=wgtd, zd=zd, outd=outd, y_out=y_out)
        if self.dev:
            self.D["ctxdump"] = self.dram("ctxdump", [128, self.nseq * 1024], F32)
            self.D["dbg"] = self.dram("dbg", [128, 8 * 1024], F32)

        WA = self.sb("WA", [128, 32768], mmdt)
        SMM = self.sb("SMM", [128, 12288], mmdt)
        SF32 = self.sb("SF32", [128, 6144])
        self.WAt, self.SMMt, self.SF32t = WA, SMM, SF32
        wb = WA
        xb = [SF32[:, 0:1024], SF32[:, 1024:2048]]
        xhat = SF32[:, 2048:3072]
        sqscr = SF32[:, 3072:4096]
        exps = SF32[:, 4096:5120]
        qs_tm = SF32[:, 5120:6144]
        xnfm = SMM[:, 0:1024]
        expk = SMM[:, 1024:2048]
        v_sb = SMM[:, 2048:3072]
        qs_fm = SMM[:, 3072:4096]
        ctx_sb = WA[:, 24576:24576 + self.nseq * 1024]
        sacc = [self.sb(f"sacc{i}", [128, 16]) for i in range(2)]
        lnsb = self.sb("lnsb", [128, 8 * 4 * 6])
        onesb = self.sb("onesb", [1, 128], mmdt)
        onecol = self.sb("onecol", [128, 2], mmdt)
        idsb = self.sb("idsb", [128, 128])
        eps_sb = self.sb("eps_sb", [128, 1])
        st1 = self.sb("st1", [128, 1])
        nm1 = self.sb("nm1", [128, 1])
        sq1 = self.sb("sq1", [128, 1])
        lv1 = self.sb("lv1", [128, 1])
        rstd1 = self.sb("rstd1", [128, 1])
        var1 = self.sb("var1", [128, 1])
        maxs = self.sb("maxs", [128, 8])
        ssum = self.sb("ssum", [128, 8])
        rsum = self.sb("rsum", [128, 8])
        ssum2 = self.sb("ssum2", [128, 8])
        rsum2 = self.sb("rsum2", [128, 8])
        rs2b = self.sb("rs2b", [128, 16])
        vfen = self.sb("vfen", [128, 8])
        rs2 = self.sb("rs2", [128, 16])
        ps = self.ctx.enter_context(nc.psum_tensor("ps", [128, 4096], F32))
        self.sbufs = locals()

        R_TR = ps[:, 0:1024]
        R_K = ps[:, 1024:2048]
        R_Q = ps[:, 2048:3072]
        R_C = ps[:, 3072:4096]

        s = {n: self.sem(n) for n in
             ["DLX", "DLW", "DST", "TT", "VC", "SC", "GP"]}

        def OP(eng, fn, waits=(), incs=()):
            P.op(eng, fn, waits, incs)

        def mark(sem):
            return P.v(sem)

        OP("vector", lambda q: q.memset(eps_sb[:], 1e-5))
        OP("sync", lambda q: q.dma_start(out=lnsb[:], in_=lncols), incs=[(s["DLW"], 16)])
        OP("sync", lambda q: q.dma_start(out=onesb[:], in_=ones1), incs=[(s["DLW"], 16)])
        OP("sync", lambda q: q.dma_start(out=idsb[:], in_=ident), incs=[(s["DLW"], 16)])
        OP("sync", lambda q: q.dma_start(out=onecol[:], in_=W["onescol"]),
           incs=[(s["DLW"], 16)])
        init_w = mark(s["DLW"])

        def phase_A(l):
            bar = [(nm_, P.v(nm_)) for nm_ in s.values()]
            for wi, nm in enumerate(["wk", "wv", "wq"]):
                src = W[nm][l].rearrange("(c p) n -> p c n", p=128)
                dst = wb[:, wi * 8192 : (wi + 1) * 8192].rearrange("p (c n) -> p c n", n=1024)
                OP("sync", lambda q, src=src, dst=dst: q.dma_start(out=dst, in_=src),
                   waits=(bar if wi == 0 else []), incs=[(s["DLW"], 16)])
            wready = mark(s["DLW"])
            wkv = lambda wi, c, n0, n1: wb[:, wi * 8192 + c * 1024 + n0 : wi * 8192 + c * 1024 + n1]

            m_xfree = [(0, 0), (0, 0)]
            m_dump = 0
            m_prev = dict(VNF=0, SEXP=0, VV=0, VSD=0, SQE=0, VQF=0, VQS=0, VCD=0,
                          TK=0, TQ=0, TV=0, TTR=0, TTR2=0, TC8=0, TS8=0, VCTX=0, RSTD=0,
                          DQS=0)
            pend_tr2 = None

            for i in range(ntiles):
                seq = i // tiles_per_seq
                first = (i % tiles_per_seq) == 0
                last = (i % tiles_per_seq) == tiles_per_seq - 1
                xcur = xb[i % 2]
                src = x_in if l == 0 else resid

                OP("sync", lambda q, xcur=xcur, sl=src[i * 128:(i + 1) * 128, :]:
                   q.dma_start(out=xcur, in_=sl),
                   waits=[(s["VC"], m_xfree[i % 2][0]), (s["SC"], m_xfree[i % 2][1]),
                          (s["DST"], m_dump)] + (bar if i == 0 else []),
                   incs=[(s["DLX"], 16)])
                m_x = mark(s["DLX"])

                OP("vector", lambda q, xcur=xcur: q.reduce_sum(st1[:], xcur, axis=AX.X,
                                                               negate=True),
                   waits=[(s["DLX"], m_x), (s["SC"], m_prev["RSTD"]),
                          (s["DST"], m_dump)], incs=[(s["VC"], 1)])
                m_sum = mark(s["VC"])
                OP("scalar", lambda q: q.mul(nm1[:], st1[:], 1.0 / 1024),
                   waits=[(s["VC"], m_sum), (s["DST"], m_dump)])
                OP("scalar", lambda q, xcur=xcur: q.activation(
                    sqscr, xcur, AF.Square, bias=st1[:], scale=1024.0,
                    accum_out=sq1[:]))
                OP("scalar", lambda q: q.activation(lv1[:], sq1[:], AF.Ln, bias=eps_sb[:],
                                                    scale=1.0 / (1024.0 ** 3)))
                OP("scalar", lambda q: q.activation(rstd1[:], lv1[:], AF.Exp, scale=-0.5),
                   incs=[(s["SC"], 1)])
                m_rstd = mark(s["SC"])
                m_prev["RSTD"] = m_rstd
                OP("vector", lambda q, xcur=xcur: q.tensor_scalar(
                    xhat, xcur, nm1[:], rstd1[:], ALU.add, ALU.mult),
                   waits=[(s["SC"], m_rstd), (s["TT"], m_prev["TTR"])], incs=[(s["VC"], 1)])
                m_xhat = mark(s["VC"])
                if self.dev and i == 0 and l == 0:
                    OP("gpsimd", lambda q: q.dma_start(out=self.D["dbg"][:, 0:1024], in_=xhat),
                       waits=[(s["VC"], m_xhat)], incs=[(s["DST"], 16)])
                    OP("gpsimd", lambda q, xcur=xcur: q.dma_start(out=self.D["dbg"][:, 6144:7168], in_=xcur),
                       waits=[(s["VC"], m_xhat)], incs=[(s["DST"], 16)])
                    OP("gpsimd", lambda q: q.dma_start(out=self.D["dbg"][:, 7168:7169], in_=st1[:], allow_slow_non_contiguous=True),
                       waits=[(s["VC"], m_xhat)], incs=[(s["DST"], 16)])
                    OP("gpsimd", lambda q: q.dma_start(out=self.D["dbg"][:, 7169:7170], in_=nm1[:], allow_slow_non_contiguous=True),
                       waits=[(s["VC"], m_xhat)], incs=[(s["DST"], 16)])
                    OP("gpsimd", lambda q: q.dma_start(out=self.D["dbg"][:, 7170:7171], in_=rstd1[:], allow_slow_non_contiguous=True),
                       waits=[(s["VC"], m_xhat)], incs=[(s["DST"], 16)])

                for c in range(8):
                    OP("tensor", lambda q, c=c: q.transpose(
                        R_TR[:, c * 128:(c + 1) * 128], xhat[:, c * 128:(c + 1) * 128], idsb[:]),
                       waits=([(s["VC"], m_xhat), (s["VC"], m_prev["VSD"]),
                               (s["DLW"], wready)] if c == 0 else []),
                       incs=([(s["TT"], 1)] if c == 7 else []))
                m_prev["TTR"] = mark(s["TT"])
                for c in range(8):
                    wc = lnsb[:, (l * 4 + 0) * 8 + c : (l * 4 + 0) * 8 + c + 1]
                    bc = lnsb[:, (l * 4 + 1) * 8 + c : (l * 4 + 1) * 8 + c + 1]
                    OP("vector", lambda q, c=c, wc=wc, bc=bc: q.tensor_scalar(
                        xnfm[:, c * 128:(c + 1) * 128], R_TR[:, c * 128:(c + 1) * 128],
                        wc, bc, ALU.mult, ALU.add),
                       waits=([(s["TT"], m_prev["TTR"]), (s["TT"], m_prev["TQ"])] if c == 0 else []),
                       incs=([(s["VC"], 1)] if c == 7 else []))
                m_prev["VNF"] = mark(s["VC"])
                if self.dev and i == 0 and l == 0:
                    OP("gpsimd", lambda q: q.dma_start(out=self.D["dbg"][:, 1024:2048].bitcast(mmdt), in_=xnfm),
                       waits=[(s["VC"], m_prev["VNF"])], incs=[(s["DST"], 16)])

                if pend_tr2 is not None:
                    pend_tr2()
                    pend_tr2 = None

                for n in range(2):
                    for c in range(8):
                        OP("tensor", lambda q, c=c, n=n: q.matmul(
                            R_K[:, n * 512:(n + 1) * 512], xnfm[:, c * 128:(c + 1) * 128],
                            wkv(0, c, n * 512, (n + 1) * 512),
                            start=(c == 0), stop=(c == 7)),
                           waits=([(s["VC"], m_prev["VNF"]), (s["SC"], m_prev["VQF"])]
                                  if (n == 0 and c == 0) else []),
                           incs=([(s["TT"], 1)] if (n == 1 and c == 7) else []))
                m_prev["TK"] = mark(s["TT"])
                OP("scalar", lambda q: q.activation(expk, ps[:, 1024:2048], AF.Exp),
                   waits=[(s["TT"], m_prev["TK"])], incs=[(s["SC"], 1)])
                m_prev["SEXP"] = mark(s["SC"])
                if self.dev and i == 0 and l == 0:
                    OP("gpsimd", lambda q: q.dma_start(out=self.D["dbg"][:, 2048:3072].bitcast(mmdt), in_=expk),
                       waits=[(s["SC"], m_prev["SEXP"])], incs=[(s["DST"], 16)])

                for n in range(2):
                    for c in range(8):
                        OP("tensor", lambda q, c=c, n=n: q.matmul(
                            R_TR[:, n * 512:(n + 1) * 512], xnfm[:, c * 128:(c + 1) * 128],
                            wkv(1, c, n * 512, (n + 1) * 512),
                            start=(c == 0), stop=(c == 7)),
                           waits=([(s["VC"], m_prev["VNF"]), (s["VC"], m_prev["VV"])]
                                  if (n == 0 and c == 0) else []),
                           incs=([(s["TT"], 1)] if (n == 1 and c == 7) else []))
                m_prev["TV"] = mark(s["TT"])
                OP("vector", lambda q: q.tensor_copy(v_sb, ps[:, 0:1024]),
                   waits=[(s["TT"], m_prev["TV"])], incs=[(s["VC"], 1)])
                m_prev["VV"] = mark(s["VC"])
                if self.dev and i == 0 and l == 0:
                    OP("gpsimd", lambda q: q.dma_start(out=self.D["dbg"][:, 3072:4096].bitcast(mmdt), in_=v_sb),
                       waits=[(s["VC"], m_prev["VV"])], incs=[(s["DST"], 16)])

                for n in range(2):
                    for c in range(8):
                        OP("tensor", lambda q, c=c, n=n: q.matmul(
                            R_Q[:, n * 512:(n + 1) * 512], xnfm[:, c * 128:(c + 1) * 128],
                            wkv(2, c, n * 512, (n + 1) * 512),
                            start=(c == 0), stop=(c == 7)),
                           waits=([(s["SC"], m_prev["SQE"])] if (n == 0 and c == 0) else []),
                           incs=([(s["TT"], 1)] if (n == 1 and c == 7) else []))
                m_prev["TQ"] = mark(s["TT"])

                for h in range(8):
                    OP("tensor", lambda q, h=h, first=first, last=last: q.matmul(
                        R_C[:, h * 128:(h + 1) * 128], expk[:, h * 128:(h + 1) * 128],
                        v_sb[:, h * 128:(h + 1) * 128],
                        start=(first and h % 4 == 0), stop=last, skip_group_check=True),
                       waits=([(s["SC"], m_prev["SEXP"]), (s["VC"], m_prev["VV"]),
                               (s["VC"], m_prev["VCTX"])] if h == 0 else []),
                       incs=([(s["TT"], 1)] if h == 7 else []))
                m_prev["TC8"] = mark(s["TT"])
                for h in range(8):
                    OP("tensor", lambda q, h=h: q.matmul(
                        R_TR[:, h * 2:h * 2 + 2], expk[:, h * 128:(h + 1) * 128], onecol[:],
                        start=True, stop=True),
                       waits=([(s["VC"], m_prev["VV"])] if h == 0 else []),
                       incs=([(s["TT"], 1)] if h == 7 else []))
                m_prev["TS8"] = mark(s["TT"])
                par = (i % tiles_per_seq) % 2
                if first:
                    OP("vector", lambda q, par=par: q.tensor_copy(sacc[par][:], ps[:, 0:16]),
                       waits=[(s["TT"], m_prev["TS8"])], incs=[(s["VC"], 1)])
                else:
                    OP("vector", lambda q, par=par: q.tensor_tensor(
                        sacc[par][:], sacc[1 - par][:], ps[:, 0:16], ALU.add),
                       waits=[(s["TT"], m_prev["TS8"])], incs=[(s["VC"], 1)])
                m_prev["VSD"] = mark(s["VC"])

                q3 = R_Q.rearrange("p (h t) -> p h t", t=128)
                OP("vector", lambda q, q3=q3: q.tensor_reduce(
                    maxs[:], q3, axis=AX.X, op=ALU.max, negate=True),
                   waits=[(s["TT"], m_prev["TQ"]), (s["SC"], m_prev["SQE"])],
                   incs=[(s["VC"], 1)])
                m_qm = mark(s["VC"])
                for h in range(8):
                    OP("scalar", lambda q, h=h: q.activation(
                        exps[:, h * 128:(h + 1) * 128], R_Q[:, h * 128:(h + 1) * 128],
                        AF.Exp, bias=maxs[:, h:h + 1], accum_out=ssum[:, h:h + 1]),
                       waits=([(s["VC"], m_qm)] if h == 0 else []),
                       incs=([(s["SC"], 1)] if h == 7 else []))
                OP("scalar", lambda q: q.copy(ssum2[:], ssum[:]), incs=[(s["SC"], 1)])
                m_prev["SQE"] = mark(s["SC"])
                OP("vector", lambda q: q.reciprocal(rsum[:], ssum2[:]),
                   waits=[(s["SC"], m_prev["SQE"])], incs=[(s["VC"], 1)])
                m_rr = mark(s["VC"])
                OP("scalar", lambda q: q.copy(rsum2[:], rsum[:]),
                   waits=[(s["VC"], m_rr)], incs=[(s["SC"], 1)])
                m_rr2 = mark(s["SC"])
                for h in range(8):
                    OP("vector", lambda q, h=h: q.tensor_scalar(
                        qs_tm[:, h * 128:(h + 1) * 128], exps[:, h * 128:(h + 1) * 128],
                        rsum2[:, h:h + 1], None, ALU.mult),
                       waits=([(s["TT"], m_prev["TTR2"]), (s["SC"], m_rr2)] if h == 0 else []),
                       incs=([(s["VC"], 1)] if h == 7 else []))
                m_prev["VQS"] = mark(s["VC"])
                if self.dev and i == 0 and l == 0:
                    OP("gpsimd", lambda q: q.dma_start(out=self.D["dbg"][:, 4096:5120], in_=qs_tm),
                       waits=[(s["VC"], m_prev["VQS"])], incs=[(s["DST"], 16)])
                    OP("gpsimd", lambda q: q.dma_start(out=self.D["dbg"][:, 5120:6144], in_=exps),
                       waits=[(s["VC"], m_prev["VQS"])], incs=[(s["DST"], 16)])
                    m_dump = mark(s["DST"])

                def make_tr2(i=i, vqs=m_prev["VQS"], tk_cur=m_prev["TK"]):
                    def emit():
                        for c in range(8):
                            OP("tensor", lambda q, c=c: q.transpose(
                                R_K[:, c * 128:(c + 1) * 128], qs_tm[:, c * 128:(c + 1) * 128],
                                idsb[:]),
                               waits=([(s["VC"], vqs)] if c == 0 else []),
                               incs=([(s["TT"], 1)] if c == 7 else []))
                        m_tr2 = mark(s["TT"])
                        m_prev["TTR2"] = m_tr2
                        for c in range(8):
                            OP("scalar", lambda q, c=c: q.copy(
                                qs_fm[:, c * 128:(c + 1) * 128], R_K[:, c * 128:(c + 1) * 128]),
                               waits=([(s["TT"], m_tr2), (s["DST"], m_prev["DQS"])]
                                      if c == 0 else []),
                               incs=([(s["SC"], 1)] if c == 7 else []))
                        m_prev["VQF"] = mark(s["SC"])
                        OP("gpsimd", lambda q, i=i: q.dma_start(
                            out=qsd[:, i], in_=qs_fm.rearrange("p (c t) -> p c t", t=128)),
                           waits=[(s["SC"], m_prev["VQF"])], incs=[(s["DST"], 16)])
                        m_prev["DQS"] = mark(s["DST"])
                    return emit
                pend_tr2 = make_tr2()

                m_xfree[i % 2] = (m_xhat, m_rstd)

                if last:
                    pend_tr2()
                    pend_tr2 = None
                    OP("vector", lambda q, par=par: q.reciprocal(rs2[:], sacc[par][:]),
                       waits=[(s["TT"], m_prev["TC8"])], incs=[(s["VC"], 1)])
                    m_rc = mark(s["VC"])
                    OP("scalar", lambda q: q.copy(rs2b[:], rs2[:]),
                       waits=[(s["VC"], m_rc)], incs=[(s["SC"], 1)])
                    m_rc2 = mark(s["SC"])
                    for h in range(8):
                        OP("vector", lambda q, h=h, seq=seq: q.tensor_scalar(
                            ctx_sb[:, seq * 1024 + h * 128: seq * 1024 + (h + 1) * 128],
                            R_C[:, h * 128:(h + 1) * 128], rs2b[:, 2 * h:2 * h + 1], None, ALU.mult),
                           waits=([(s["SC"], m_rc2)] if h == 0 else []),
                           incs=([(s["VC"], 1)] if h == 7 else []))
                    m_prev["VCTX"] = mark(s["VC"])
            if self.dev:
                OP("gpsimd", lambda q: q.dma_start(
                    out=self.D["ctxdump"], in_=ctx_sb),
                   waits=[(s["VC"], m_prev["VCTX"])], incs=[(s["DST"], 16)])
            self.s = s
            return m_prev

        self.phase_A = phase_A

        def phase_B(l):
            bar = [(nm_, P.v(nm_)) for nm_ in s.values()]
            woW = WA[:, 0:8192]
            xn2out = WA[:, 8192:9216]
            qsb = SMM[:, 0:4096]
            attnfm = SMM[:, 4096:8192]
            xm = [SF32[:, 0:1024], SF32[:, 1024:2048]]
            x2m = [SF32[:, 2048:3072], SF32[:, 3072:4096]]
            sqB = SF32[:, 4096:5120]
            bo_b = SF32[:, 5120:6144]
            pA = lambda h: ps[:, (h % 2) * 512:(h % 2) * 512 + 512]
            pY = lambda m: ps[:, 1024 + (m % 2) * 1024: 1024 + (m % 2) * 1024 + 1024]
            pT = ps[:, 3072:4096]
            src = x_in if l == 0 else resid

            OP("sync", lambda q: q.dma_start(
                out=woW.rearrange("p (c n) -> p c n", n=1024),
                in_=W["wo"][l].rearrange("(c p) n -> p c n", p=128)),
               waits=bar, incs=[(s["DLW"], 16)])
            OP("sync", lambda q: q.dma_start(
                out=bo_b, in_=W["bo"][l:l + 1, :].to_broadcast([128, 1024])),
               incs=[(s["DLW"], 16)])
            m_w = mark(s["DLW"])

            mk = dict(yTT=0, aSC=0, stVC=[0, 0], trTT=[0, 0], DQS=0, DX2=[0, 0],
                      x2VC=[0, 0], x2SC=[0, 0], TRD=0, RSTD=0)
            first_v = [True]
            first_s = [True]

            for j in range(NTOK // 512):
                for c_ in range(8):
                    OP("sync", lambda q, j=j, c_=c_: q.dma_start(
                        out=qsb[:, c_ * 512:(c_ + 1) * 512].rearrange("p (t k) -> p t k", k=128),
                        in_=qsd[:, 4 * j:4 * j + 4, c_, :]),
                       waits=(bar + [(s["TT"], mk["yTT"])] if c_ == 0 else []),
                       incs=[(s["DLX"], 16)])
                m_qs = mark(s["DLX"])
                seq = (j * 512) // T
                evac_marks = {}
                for h in range(8):
                    w_h = [(s["DLX"], m_qs), (s["DLW"], m_w)] if h == 0 else (
                        [(s["SC"], evac_marks[h - 2])] if h >= 2 else [])
                    OP("tensor", lambda q, h=h, seq=seq: q.matmul(
                        pA(h), ctx_sb[:, seq * 1024 + h * 128:seq * 1024 + (h + 1) * 128],
                        qsb[:, h * 512:(h + 1) * 512], start=True, stop=True),
                       waits=w_h, incs=[(s["TT"], 1)])
                    m_mm = mark(s["TT"])
                    OP("scalar", lambda q, h=h: q.copy(
                        attnfm[:, h * 512:(h + 1) * 512], pA(h)),
                       waits=[(s["TT"], m_mm)] + (bar if first_s[0] else []),
                       incs=[(s["SC"], 1)])
                    first_s[0] = False
                    evac_marks[h] = mark(s["SC"])
                m_attn = mark(s["SC"])

                for m in range(4):
                    i128 = 4 * j + m
                    OP("sync", lambda q, m=m, i128=i128: q.dma_start(
                        out=xm[m % 2], in_=src[i128 * 128:(i128 + 1) * 128, :]),
                       waits=[(s["TT"], mk["trTT"][m % 2])], incs=[(s["DLX"], 16)])
                    m_x = mark(s["DLX"])
                    for n in range(2):
                        for h in range(8):
                            OP("tensor", lambda q, m=m, n=n, h=h: q.matmul(
                                pY(m)[:, n * 512:(n + 1) * 512],
                                attnfm[:, h * 512 + m * 128: h * 512 + (m + 1) * 128],
                                woW[:, h * 1024 + n * 512: h * 1024 + (n + 1) * 512],
                                start=(h == 0), stop=(h == 7)),
                               waits=([(s["SC"], m_attn), (s["VC"], mk["x2VC"][m % 2])]
                                      if (n == 0 and h == 0) else []),
                               incs=([(s["TT"], 1)] if (n == 1 and h == 7) else []))
                    mk["yTT"] = mark(s["TT"])
                    OP("vector", lambda q, m=m: q.tensor_tensor(sqB, pY(m), bo_b, ALU.add),
                       waits=[(s["TT"], mk["yTT"]), (s["DLX"], m_x),
                              (s["SC"], mk["x2SC"][m % 2]), (s["DST"], mk["DX2"][m % 2])]
                             + (bar if first_v[0] else []))
                    first_v[0] = False
                    OP("vector", lambda q, m=m: q.tensor_tensor(
                        x2m[m % 2], sqB, xm[m % 2], ALU.add))
                    OP("vector", lambda q, m=m: q.tensor_copy(
                        vfen[:], x2m[m % 2][:, 1016:1024]), incs=[(s["VC"], 1)])
                    mk["x2VC"][m % 2] = mark(s["VC"])
                    OP("gpsimd", lambda q, m=m, i128=i128: q.dma_start(
                        out=resid[i128 * 128:(i128 + 1) * 128, :], in_=x2m[m % 2]),
                       waits=[(s["VC"], mk["x2VC"][m % 2])], incs=[(s["DST"], 16)])
                    mk["DX2"][m % 2] = mark(s["DST"])
                    OP("vector", lambda q, m=m: q.reduce_sum(
                        st1[:], x2m[m % 2], axis=AX.X, negate=True),
                       waits=[(s["SC"], mk["RSTD"])], incs=[(s["VC"], 1)])
                    m_sum = mark(s["VC"])
                    OP("scalar", lambda q: q.mul(nm1[:], st1[:], 1.0 / 1024),
                       waits=[(s["VC"], m_sum)])
                    OP("scalar", lambda q, m=m: q.activation(
                        sqB, x2m[m % 2], AF.Square, bias=st1[:], scale=1024.0,
                        accum_out=sq1[:]))
                    OP("scalar", lambda q: q.activation(lv1[:], sq1[:], AF.Ln, bias=eps_sb[:],
                                                        scale=1.0 / (1024.0 ** 3)))
                    OP("scalar", lambda q: q.activation(rstd1[:], lv1[:], AF.Exp, scale=-0.5),
                       incs=[(s["SC"], 1)])
                    mk["RSTD"] = mark(s["SC"])
                    mk["x2SC"][m % 2] = mk["RSTD"]
                    OP("vector", lambda q, m=m: q.tensor_scalar(
                        xm[m % 2], x2m[m % 2], nm1[:], rstd1[:], ALU.add, ALU.mult),
                       waits=[(s["SC"], mk["RSTD"]), (s["TT"], mk["trTT"][m % 2])],
                       incs=[(s["VC"], 1)])
                    m_hat = mark(s["VC"])
                    for c in range(8):
                        OP("tensor", lambda q, c=c, m=m: q.transpose(
                            pT[:, c * 128:(c + 1) * 128], xm[m % 2][:, c * 128:(c + 1) * 128],
                            idsb[:]),
                           waits=([(s["VC"], m_hat), (s["VC"], mk["TRD"])] if c == 0 else []),
                           incs=([(s["TT"], 1)] if c == 7 else []))
                    mk["trTT"][m % 2] = mark(s["TT"])
                    for c in range(8):
                        wc = lnsb[:, (l * 4 + 2) * 8 + c:(l * 4 + 2) * 8 + c + 1]
                        bc = lnsb[:, (l * 4 + 3) * 8 + c:(l * 4 + 3) * 8 + c + 1]
                        OP("vector", lambda q, c=c, wc=wc, bc=bc: q.tensor_scalar(
                            xn2out[:, c * 128:(c + 1) * 128], pT[:, c * 128:(c + 1) * 128],
                            wc, bc, ALU.mult, ALU.add),
                           waits=([(s["TT"], mk["trTT"][m % 2]), (s["DST"], mk["DQS"])]
                                  if c == 0 else []),
                           incs=([(s["VC"], 1)] if c == 7 else []))
                    mk["TRD"] = mark(s["VC"])
                    OP("gpsimd", lambda q, i128=i128: q.dma_start(
                        out=xn2fmd[:, i128], in_=xn2out.rearrange("p (c t) -> p c t", t=128)),
                       waits=[(s["VC"], mk["TRD"])], incs=[(s["DST"], 16)])
                    mk["DQS"] = mark(s["DST"])

        self.phase_B = phase_B

        def phase_C(l, F0):
            bar = [(nm_, P.v(nm_)) for nm_ in s.values()]
            w1h = WA[:, 0:16384]
            w2h = WA[:, 16384:32768]
            xn2sb = SMM[:, 0:4096]
            hdnfm = SMM[:, 4096:12288]
            tl = SF32[:, 0:512]
            stage = [SF32[:, 1024:2048], SF32[:, 2048:3072]]
            pH = lambda fs: ps[:, (fs % 4) * 512:(fs % 4) * 512 + 512]
            pY2 = lambda m_: ps[:, 2048 + (m_ % 2) * 1024: 2048 + (m_ % 2) * 1024 + 1024]

            OP("sync", lambda q: q.dma_start(
                out=w1h.rearrange("p (c n) -> p c n", n=2048),
                in_=W["w1"][l, :, F0:F0 + 2048].rearrange("(c p) n -> p c n", p=128)),
               waits=bar, incs=[(s["DLW"], 16)])
            OP("sync", lambda q: q.dma_start(
                out=w2h.rearrange("p (k n) -> p k n", n=1024),
                in_=W["w2"][l, F0:F0 + 2048, :].rearrange("(k p) n -> p k n", p=128)),
               incs=[(s["DLW"], 16)])
            m_w = mark(s["DLW"])

            mk = dict(w2TT=0, lkVC=[0, 0, 0, 0], drSC=[0, 0], drDST=[0, 0], hTT={})
            firsts = {"vector": [bar], "scalar": [bar], "gpsimd": [bar]}

            def fw(eng):
                if firsts.get(eng) and firsts[eng]:
                    return firsts[eng].pop()
                return []

            for j in range(NTOK // 512):
                for c_ in range(8):
                    OP("sync", lambda q, j=j, c_=c_: q.dma_start(
                        out=xn2sb[:, c_ * 512:(c_ + 1) * 512].rearrange("p (t k) -> p t k", k=128),
                        in_=xn2fmd[:, 4 * j:4 * j + 4, c_, :]),
                       waits=([(s["TT"], mk["hTT"].get(15, 0))] if c_ == 0 else []),
                       incs=[(s["DLX"], 16)])
                m_xn = mark(s["DLX"])
                for fs in range(16):
                    for c_ in range(8):
                        OP("tensor", lambda q, fs=fs, c_=c_: q.matmul(
                            pH(fs), w1h[:, c_ * 2048 + fs * 128: c_ * 2048 + (fs + 1) * 128],
                            xn2sb[:, c_ * 512:(c_ + 1) * 512],
                            start=(c_ == 0), stop=(c_ == 7)),
                           waits=([(s["DLX"], m_xn), (s["DLW"], m_w),
                                   (s["VC"], mk["lkVC"][fs % 4])] if c_ == 0 else []),
                           incs=([(s["TT"], 1)] if c_ == 7 else []))
                    m_h = mark(s["TT"])
                    mk["hTT"][fs] = m_h
                    OP("scalar", lambda q, fs=fs: q.mul(tl, pH(fs), 0.01),
                       waits=[(s["TT"], m_h)] + fw("scalar"), incs=[(s["SC"], 1)])
                    m_tl = mark(s["SC"])
                    OP("vector", lambda q, fs=fs: q.tensor_tensor(
                        hdnfm[:, fs * 512:(fs + 1) * 512], pH(fs), tl, ALU.max),
                       waits=[(s["SC"], m_tl), (s["TT"], mk["w2TT"])] + fw("vector"),
                       incs=[(s["VC"], 1)])
                    mk["lkVC"][fs % 4] = mark(s["VC"])
                m_leaky = mark(s["VC"])
                for m_ in range(4):
                    i128 = 4 * j + m_
                    for n in range(2):
                        for k_ in range(16):
                            OP("tensor", lambda q, m_=m_, n=n, k_=k_: q.matmul(
                                pY2(m_)[:, n * 512:(n + 1) * 512],
                                hdnfm[:, k_ * 512 + m_ * 128: k_ * 512 + (m_ + 1) * 128],
                                w2h[:, k_ * 1024 + n * 512: k_ * 1024 + (n + 1) * 512],
                                start=(k_ == 0), stop=(k_ == 15)),
                               waits=([(s["VC"], m_leaky), (s["SC"], mk["drSC"][m_ % 2])]
                                      if (n == 0 and k_ == 0) else []),
                               incs=([(s["TT"], 1)] if (n == 1 and k_ == 15) else []))
                    m_y2 = mark(s["TT"])
                    if m_ == 3:
                        mk["w2TT"] = m_y2
                    OP("scalar", lambda q, m_=m_: q.copy(stage[m_ % 2], pY2(m_)),
                       waits=[(s["TT"], m_y2), (s["DST"], mk["drDST"][m_ % 2])],
                       incs=[(s["SC"], 1)])
                    mk["drSC"][m_ % 2] = mark(s["SC"])
                    OP("gpsimd", lambda q, m_=m_, i128=i128: q.dma_start(
                        out=resid[i128 * 128:(i128 + 1) * 128, :], in_=stage[m_ % 2],
                        accum_op=ALU.add),
                       waits=[(s["SC"], mk["drSC"][m_ % 2])] + fw("gpsimd"),
                       incs=[(s["DST"], 16)])
                    mk["drDST"][m_ % 2] = mark(s["DST"])

        self.phase_C = phase_C

        def phase_T():
            bar = [(nm_, P.v(nm_)) for nm_ in s.values()]
            a1W = WA[:, 0:3072]
            a3W = WA[:, 3072:6144]
            gW = WA[:, 6144:6208]
            xtout = WA[:, 6656:7680]
            t1out = WA[:, 7680:8064]
            t3out = WA[:, 8064:8448]
            wfm = WA[:, 8448:8704]
            xm = [SF32[:, 0:1024], SF32[:, 1024:2048]]
            x2m = [SF32[:, 2048:3072], SF32[:, 3072:4096]]
            wpad = SF32[:, 4096:4352]
            t1s = SF32[:, 4608:4992]
            t3s = SF32[:, 5632:6016]
            top8 = SF32[:, 5120:5128]
            top8b = SF32[:, 5128:5136]
            e8 = SF32[:, 5136:5144]
            msk = SF32[:, 5144:5152]
            wnum = SF32[:, 5152:5160]
            w_tm = SF32[:, 5160:5168]
            negv1 = SF32[:, 5168:5169]
            ev2 = SF32[:, 5169:5170]
            den = SF32[:, 5176:5184]
            rden = SF32[:, 5184:5192]
            rdenb = SF32[:, 5192:5200]
            pT = ps[:, 0:1024]
            pG = ps[:, 1024:1032]
            pt1 = ps[:, 1536:1920]
            pt3 = ps[:, 2560:2944]
            pW = ps[:, 2048:2560]

            OP("sync", lambda q: q.dma_start(
                out=a1W.rearrange("p (c n) -> p c n", n=384),
                in_=W["A1cat"].rearrange("(c p) n -> p c n", p=128)),
               waits=bar, incs=[(s["DLW"], 16)])
            OP("sync", lambda q: q.dma_start(
                out=a3W.rearrange("p (c n) -> p c n", n=384),
                in_=W["A3cat"].rearrange("(c p) n -> p c n", p=128)),
               incs=[(s["DLW"], 16)])
            OP("sync", lambda q: q.dma_start(
                out=gW.rearrange("p (c n) -> p c n", n=8),
                in_=W["gate_wT"].rearrange("(c p) n -> p c n", p=128)),
               incs=[(s["DLW"], 16)])
            m_w = mark(s["DLW"])
            OP("vector", lambda q: q.memset(wpad, 0.0), waits=bar)
            first_s = [bar]
            first_g = [bar]
            mk = dict(RSTD=0, trTT=[0, 0], evSC=0, DXT=0, mmTT=0, WEV=0, T2=0, MSK=0, WT=0,
                      xD=[0, 0], x2V=[0, 0], x2S=[0, 0])

            for i128 in range(ntiles):
                m = i128 % 2
                OP("sync", lambda q, i128=i128, m=m: q.dma_start(
                    out=xm[m], in_=resid[i128 * 128:(i128 + 1) * 128, :]),
                   waits=[(s["TT"], mk["trTT"][m]), (s["DST"], mk["xD"][m])]
                         + (bar if i128 == 0 else []),
                   incs=[(s["DLX"], 16)])
                OP("sync", lambda q, i128=i128, m=m: q.dma_start(
                    out=x2m[m], in_=x_in[i128 * 128:(i128 + 1) * 128, :]),
                   waits=[(s["VC"], mk["x2V"][m]), (s["SC"], mk["x2S"][m])],
                   incs=[(s["DLX"], 16)])
                m_x = mark(s["DLX"])
                OP("vector", lambda q, m=m: q.tensor_tensor(x2m[m], xm[m], x2m[m], ALU.add),
                   waits=[(s["DLX"], m_x), (s["SC"], mk["RSTD"])], incs=[(s["VC"], 1)])
                OP("vector", lambda q, m=m: q.reduce_sum(st1[:], x2m[m], axis=AX.X,
                                                         negate=True), incs=[(s["VC"], 1)])
                m_sum = mark(s["VC"])
                OP("scalar", lambda q: q.mul(nm1[:], st1[:], 1.0 / 1024),
                   waits=[(s["VC"], m_sum)] + (first_s.pop() if first_s else []))
                OP("scalar", lambda q, m=m: q.activation(
                    xm[m], x2m[m], AF.Square, bias=st1[:], scale=1024.0, accum_out=sq1[:]))
                OP("scalar", lambda q: q.activation(lv1[:], sq1[:], AF.Ln, bias=eps_sb[:],
                                                    scale=1.0 / (1024.0 ** 3)))
                OP("scalar", lambda q: q.activation(rstd1[:], lv1[:], AF.Exp, scale=-0.5),
                   incs=[(s["SC"], 1)])
                mk["RSTD"] = mark(s["SC"])
                mk["x2S"][m] = mk["RSTD"]
                OP("vector", lambda q, m=m: q.tensor_scalar(
                    xm[m], x2m[m], nm1[:], rstd1[:], ALU.add, ALU.mult),
                   waits=[(s["SC"], mk["RSTD"]), (s["TT"], mk["trTT"][m])],
                   incs=[(s["VC"], 1)])
                OP("vector", lambda q, m=m: q.tensor_copy(
                    vfen[:], xm[m][:, 1016:1024]), incs=[(s["VC"], 1)])
                m_hat = mark(s["VC"])
                mk["x2V"][m] = m_hat
                OP("gpsimd", lambda q, i128=i128, m=m: q.dma_start(
                    out=xtd[i128 * 128:(i128 + 1) * 128, :], in_=xm[m]),
                   waits=[(s["VC"], m_hat)] + (first_g.pop() if first_g else []),
                   incs=[(s["DST"], 16)])
                mk["xD"][m] = mark(s["DST"])
                for c in range(8):
                    OP("tensor", lambda q, c=c, m=m: q.transpose(
                        pT[:, c * 128:(c + 1) * 128], xm[m][:, c * 128:(c + 1) * 128], idsb[:]),
                       waits=([(s["VC"], m_hat), (s["SC"], mk["T2"]),
                               (s["DLW"], m_w)] if c == 0 else []),
                       incs=([(s["TT"], 1)] if c == 7 else []))
                mk["trTT"][m] = mark(s["TT"])
                for c in range(8):
                    OP("scalar", lambda q, c=c: q.copy(
                        xtout[:, c * 128:(c + 1) * 128], pT[:, c * 128:(c + 1) * 128]),
                       waits=([(s["TT"], mk["trTT"][m]), (s["DST"], mk["DXT"]),
                               (s["TT"], mk["mmTT"])] if c == 0 else []),
                       incs=([(s["SC"], 1)] if c == 7 else []))
                mk["evSC"] = mark(s["SC"])
                OP("gpsimd", lambda q, i128=i128: q.dma_start(
                    out=xtfmd[:, i128], in_=xtout.rearrange("p (c t) -> p c t", t=128)),
                   waits=[(s["SC"], mk["evSC"])], incs=[(s["DST"], 16)])
                mk["DXT"] = mark(s["DST"])
                for c in range(8):
                    OP("tensor", lambda q, c=c: q.matmul(
                        pG, xtout[:, c * 128:(c + 1) * 128], gW[:, c * 8:(c + 1) * 8],
                        start=(c == 0), stop=(c == 7)),
                       waits=([(s["SC"], mk["evSC"]), (s["VC"], mk["MSK"])] if c == 0 else []))
                for c in range(8):
                    OP("tensor", lambda q, c=c: q.matmul(
                        pt1, xtout[:, c * 128:(c + 1) * 128], a1W[:, c * 384:(c + 1) * 384],
                        start=(c == 0), stop=(c == 7)))
                for c in range(8):
                    OP("tensor", lambda q, c=c: q.matmul(
                        pt3, xtout[:, c * 128:(c + 1) * 128], a3W[:, c * 384:(c + 1) * 384],
                        start=(c == 0), stop=(c == 7)),
                       incs=([(s["TT"], 1)] if c == 7 else []))
                mk["mmTT"] = mark(s["TT"])
                OP("vector", lambda q: q.max(out=top8, in_=pG),
                   waits=[(s["TT"], mk["mmTT"])], incs=[(s["VC"], 1)])
                m_m8 = mark(s["VC"])
                OP("scalar", lambda q: q.copy(top8b, top8), waits=[(s["VC"], m_m8)])
                OP("scalar", lambda q: q.mul(negv1, top8b[:, 0:1], -1.0))
                OP("scalar", lambda q: q.copy(t1s, pt1))
                OP("scalar", lambda q: q.copy(t3s, pt3), incs=[(s["SC"], 1)])
                m_tev = mark(s["SC"])
                OP("scalar", lambda q: q.activation(e8, pG, AF.Exp, bias=negv1))
                OP("scalar", lambda q: q.activation(rdenb[:, 0:1], top8b[:, 1:2], AF.Sigmoid,
                                                    scale=-1.0, bias=top8b[:, 0:1]),
                   incs=[(s["SC"], 1)])
                m_rdb = mark(s["SC"])
                OP("vector", lambda q: q.tensor_scalar(msk, pG, top8b[:, 1:2], None, ALU.is_ge),
                   waits=[(s["SC"], m_rdb)], incs=[(s["VC"], 1)])
                mk["MSK"] = mark(s["VC"])
                OP("vector", lambda q: q.scalar_tensor_tensor(
                    w_tm, e8, rdenb[:, 0:1], msk, ALU.mult, ALU.mult))
                OP("vector", lambda q: q.tensor_copy(den, w_tm))
                for e in range(8):
                    OP("vector", lambda q, e=e: q.tensor_copy(
                        wpad[:, 32 * e:32 * e + 1], w_tm[:, e:e + 1]),
                       waits=([(s["TT"], mk["WT"])] if e == 0 else []),
                       incs=([(s["VC"], 1)] if e == 7 else []))
                m_wp = mark(s["VC"])
                for j2 in range(2):
                    OP("tensor", lambda q, j2=j2: q.transpose(
                        pW[:, j2 * 128:(j2 + 1) * 128], wpad[:, j2 * 128:(j2 + 1) * 128],
                        idsb[:]),
                       waits=([(s["VC"], m_wp), (s["DST"], mk["WEV"]),
                               (s["SC"], mk["T2"])] if j2 == 0 else []),
                       incs=([(s["TT"], 1)] if j2 == 1 else []))
                m_wt = mark(s["TT"])
                mk["WT"] = m_wt
                OP("scalar", lambda q: q.copy(wfm, pW[:, 0:256]), waits=[(s["TT"], m_wt)],
                   incs=[(s["SC"], 1)])
                m_wf = mark(s["SC"])
                for e in range(8):
                    j2, i4 = divmod(e, 4)
                    OP("sync", lambda q, e=e, j2=j2, i4=i4, i128=i128: q.dma_start(
                        out=wgtd[e:e + 1, i128 * 128:(i128 + 1) * 128],
                        in_=wfm[32 * i4:32 * i4 + 1, j2 * 128:(j2 + 1) * 128].bitcast(F32)),
                       waits=([(s["SC"], m_wf)] if e == 0 else []), incs=[(s["DST"], 16)])
                for g in range(3):
                    OP("tensor", lambda q, g=g: q.transpose(
                        pt1[:, g * 128:(g + 1) * 128],
                        t1s[:, g * 128:(g + 1) * 128], idsb[:]),
                       waits=([(s["SC"], m_tev), (s["SC"], m_wf)] if g == 0 else []))
                for g in range(3):
                    OP("tensor", lambda q, g=g: q.transpose(
                        pt3[:, g * 128:(g + 1) * 128],
                        t3s[:, g * 128:(g + 1) * 128], idsb[:]),
                       incs=([(s["TT"], 1)] if g == 2 else []))
                m_t2 = mark(s["TT"])
                OP("scalar", lambda q: q.copy(t1out, pt1[:, 0:384]),
                   waits=[(s["TT"], m_t2), (s["DST"], mk["WEV"])])
                OP("scalar", lambda q: q.copy(t3out, pt3[:, 0:384]), incs=[(s["SC"], 1)])
                mk["T2"] = mark(s["SC"])
                if self.dev and i128 == 0:
                    OP("gpsimd", lambda q: q.dma_start(out=self.D["dbg"][:, 0:8], in_=top8),
                       waits=[(s["SC"], mk["T2"])], incs=[(s["DST"], 16)])
                    OP("gpsimd", lambda q: q.dma_start(out=self.D["dbg"][:, 8:16], in_=e8),
                       incs=[(s["DST"], 16)])
                    OP("gpsimd", lambda q: q.dma_start(out=self.D["dbg"][:, 16:24], in_=msk),
                       incs=[(s["DST"], 16)])
                    OP("gpsimd", lambda q: q.dma_start(out=self.D["dbg"][:, 24:32], in_=w_tm),
                       incs=[(s["DST"], 16)])
                    OP("gpsimd", lambda q: q.dma_start(out=self.D["dbg"][:, 32:288],
                                                       in_=wfm.bitcast(F32)),
                       incs=[(s["DST"], 16)])
                    OP("gpsimd", lambda q: q.dma_start(out=self.D["dbg"][:, 288:544],
                                                       in_=wpad),
                       incs=[(s["DST"], 16)])
                    OP("gpsimd", lambda q: q.dma_start(out=self.D["dbg"][:, 544:552],
                                                       in_=top8b),
                       incs=[(s["DST"], 16)])
                    OP("gpsimd", lambda q: q.dma_start(out=self.D["dbg"][:, 552:553],
                                                       in_=negv1, allow_slow_non_contiguous=True),
                       incs=[(s["DST"], 16)])
                    OP("gpsimd", lambda q: q.dma_start(out=self.D["dbg"][:, 553:554],
                                                       in_=rdenb[:, 0:1], allow_slow_non_contiguous=True),
                       incs=[(s["DST"], 16)])
                    OP("gpsimd", lambda q: q.dma_start(out=self.D["dbg"][:, 554:555],
                                                       in_=ev2, allow_slow_non_contiguous=True),
                       incs=[(s["DST"], 16)])
                    OP("gpsimd", lambda q: q.dma_start(out=self.D["dbg"][:, 555:556],
                                                       in_=den[:, 0:1], allow_slow_non_contiguous=True),
                       incs=[(s["DST"], 16)])
                    OP("gpsimd", lambda q: q.dma_start(out=self.D["dbg"][:, 557:565],
                                                       in_=wnum),
                       incs=[(s["DST"], 16)])
                OP("gpsimd", lambda q, i128=i128: q.dma_start(
                    out=t1pd[:, i128], in_=t1out.rearrange("p (j t) -> p j t", t=128)),
                   waits=[(s["SC"], mk["T2"])], incs=[(s["DST"], 16)])
                OP("gpsimd", lambda q, i128=i128: q.dma_start(
                    out=t3pd[:, i128], in_=t3out.rearrange("p (j t) -> p j t", t=128)),
                   incs=[(s["DST"], 16)])
                mk["WEV"] = mark(s["DST"])

        def phase_F(fc):
            bar = [(nm_, P.v(nm_)) for nm_ in s.values()]
            F0 = fc * 512
            wgT = WA[:, 0:4096]
            wuT = WA[:, 4096:8192]
            wdT = WA[:, 8192:12288]
            B1c = WA[:, 12288:13824]
            B3c = WA[:, 13824:15360]
            A2c = WA[:, 15360:15872]
            t1p_s = WA[:, 15872:17408]
            t3p_s = WA[:, 17408:18944]
            act2 = [WA[:, 18944:19456], WA[:, 19456:19968]]
            xtfm_s = SMM[:, 0:4096]
            mix = SMM[:, 4096:6144]
            se2 = [SMM[:, 6144:6656], SMM[:, 6656:7168]]
            be2 = [SMM[:, 7168:7680], SMM[:, 7680:8192]]
            wb = SMM[:, 8192:12288]
            u2 = [SF32[:, 0:512], SF32[:, 512:1024]]
            stage_o = [SF32[:, 1024:2048], SF32[:, 2048:3072]]
            stz = SF32[:, 3072:3584]
            zscr = SF32[:, 3584:4096]
            cw1 = SF32[:, 4096:4608]
            cw3 = SF32[:, 4608:5120]
            pcw1 = ps[:, 0:512]
            pcw3 = ps[:, 512:1024]
            pd1 = lambda ei: ps[:, 1024 + (ei % 2) * 1024: 1024 + (ei % 2) * 1024 + 512]
            pd3 = lambda ei: ps[:, 1536 + (ei % 2) * 1024: 1536 + (ei % 2) * 1024 + 512]
            pz2 = lambda mc: ps[:, 3072 + mc * 128: 3072 + (mc + 1) * 128]
            pO = ps[:, 3584:4096]

            for nm_, dstw, srcw, n_ in [
                ("wgT", wgT, W["w_gateT"][:, F0:F0 + 512], 512),
                ("wuT", wuT, W["w_upT"][:, F0:F0 + 512], 512),
            ]:
                OP("sync", lambda q, dstw=dstw, srcw=srcw, n_=n_: q.dma_start(
                    out=dstw.rearrange("p (c n) -> p c n", n=n_),
                    in_=srcw.rearrange("(c p) n -> p c n", p=128)),
                   waits=(bar if nm_ == "wgT" else []), incs=[(s["DLW"], 16)])
            OP("sync", lambda q: q.dma_start(
                out=wdT.rearrange("p (k n) -> p k n", n=1024),
                in_=W["w_downT"][F0:F0 + 512, :].rearrange("(k p) n -> p k n", p=128)),
               incs=[(s["DLW"], 16)])
            OP("sync", lambda q: q.dma_start(
                out=B1c.rearrange("p (g n) -> p g n", n=512),
                in_=W["B1cat"][:, :, F0:F0 + 512].rearrange("g p n -> p g n")),
               incs=[(s["DLW"], 16)])
            OP("sync", lambda q: q.dma_start(
                out=B3c.rearrange("p (g n) -> p g n", n=512),
                in_=W["B3cat"][:, :, F0:F0 + 512].rearrange("g p n -> p g n")),
               incs=[(s["DLW"], 16)])
            OP("sync", lambda q: q.dma_start(
                out=A2c.rearrange("p (c n) -> p c n", n=128),
                in_=W["A2T"][F0:F0 + 512, :].rearrange("(c p) n -> p c n", p=128)),
               incs=[(s["DLW"], 16)])
            m_w = mark(s["DLW"])
            first = {k: [bar] for k in ("vector", "scalar", "gpsimd", "tensor")}

            def fw(k):
                return first[k].pop() if first[k] else []

            mk = dict(eDone=0, abV={}, aSC={}, mixV={}, zTT={}, uGP={}, actGP={},
                      OdD=[0, 0], stzD=0, wbGP=0, cwSC={})

            for j in range(NTOK // 512):
                for c_ in range(8):
                    OP("sync", lambda q, j=j, c_=c_: q.dma_start(
                        out=xtfm_s[:, c_ * 512:(c_ + 1) * 512].rearrange(
                            "p (t k) -> p t k", k=128),
                        in_=xtfmd[:, 4 * j:4 * j + 4, c_, :]),
                       waits=([(s["TT"], mk["eDone"])] if c_ == 0 else []),
                       incs=[(s["DLX"], 16)])
                for g in range(3):
                    OP("sync", lambda q, j=j, g=g: q.dma_start(
                        out=t1p_s[:, g * 512:(g + 1) * 512].rearrange(
                            "p (t k) -> p t k", k=128),
                        in_=t1pd[:, 4 * j:4 * j + 4, g, :]), incs=[(s["DLX"], 16)])
                    OP("sync", lambda q, j=j, g=g: q.dma_start(
                        out=t3p_s[:, g * 512:(g + 1) * 512].rearrange(
                            "p (t k) -> p t k", k=128),
                        in_=t3pd[:, 4 * j:4 * j + 4, g, :]), incs=[(s["DLX"], 16)])
                for e in range(8):
                    OP("gpsimd", lambda q, e=e, j=j: q.dma_start(
                        out=wb[:, e * 512:(e + 1) * 512],
                        in_=wgtd[e:e + 1, j * 512:(j + 1) * 512].to_broadcast([128, 512])),
                       waits=([(s["GP"], mk["wbGP"])] if e == 0 else []),
                       incs=[(s["DLX"], 16)])
                m_ld = mark(s["DLX"])
                for fsub in range(4):
                    for c_ in range(8):
                        OP("tensor", lambda q, c_=c_, fsub=fsub: q.matmul(
                            pcw1, wgT[:, c_ * 512 + fsub * 128: c_ * 512 + (fsub + 1) * 128],
                            xtfm_s[:, c_ * 512:(c_ + 1) * 512],
                            start=(c_ == 0), stop=(c_ == 7)),
                           waits=(([(s["DLX"], m_ld), (s["DLW"], m_w),
                                    (s["SC"], mk["cwSC"].get(fsub - 1, 0))]
                                   + fw("tensor")) if c_ == 0 else []))
                    for c_ in range(8):
                        OP("tensor", lambda q, c_=c_, fsub=fsub: q.matmul(
                            pcw3, wuT[:, c_ * 512 + fsub * 128: c_ * 512 + (fsub + 1) * 128],
                            xtfm_s[:, c_ * 512:(c_ + 1) * 512],
                            start=(c_ == 0), stop=(c_ == 7)),
                           incs=([(s["TT"], 1)] if c_ == 7 else []))
                    m_cw = mark(s["TT"])
                    OP("scalar", lambda q: q.copy(cw1, pcw1),
                       waits=[(s["TT"], m_cw),
                              (s["VC"], mk["abV"].get((fsub - 1, 7), 0))] + fw("scalar"))
                    OP("scalar", lambda q: q.copy(cw3, pcw3), incs=[(s["SC"], 1)])
                    mk["cwSC"][fsub] = mark(s["SC"])
                    m_cwe = mk["cwSC"][fsub]
                    for e in range(8):
                        ei = fsub * 8 + e
                        g, pos = divmod(e, 3)
                        prev2 = (fsub, e - 2) if e >= 2 else (fsub - 1, e + 6)
                        OP("tensor", lambda q, g=g, pos=pos, fsub=fsub, ei=ei: q.matmul(
                            pd1(ei),
                            B1c[32 * pos:32 * pos + 16,
                                g * 512 + fsub * 128: g * 512 + (fsub + 1) * 128],
                            t1p_s[32 * pos:32 * pos + 16, g * 512:(g + 1) * 512],
                            start=True, stop=True),
                           waits=[(s["VC"], mk["abV"].get(prev2, 0)),
                                  (s["SC"], m_cwe if e == 0 else 0)])
                        OP("tensor", lambda q, g=g, pos=pos, fsub=fsub, ei=ei: q.matmul(
                            pd3(ei),
                            B3c[32 * pos:32 * pos + 16,
                                g * 512 + fsub * 128: g * 512 + (fsub + 1) * 128],
                            t3p_s[32 * pos:32 * pos + 16, g * 512:(g + 1) * 512],
                            start=True, stop=True), incs=[(s["TT"], 1)])
                        m_d = mark(s["TT"])
                        mk["zTT"][(fsub, e)] = 0
                        OP("vector", lambda q, ei=ei: q.scalar_tensor_tensor(
                            se2[ei % 2], pd1(ei), 2.0, cw1, ALU.mult, ALU.add),
                           waits=[(s["TT"], m_d), (s["SC"], mk["aSC"].get(prev2, 0)),
                                  (s["GP"], mk["actGP"].get(prev2, 0))] + fw("vector"))
                        OP("vector", lambda q, ei=ei: q.scalar_tensor_tensor(
                            be2[ei % 2], pd3(ei), 2.0, cw3, ALU.mult, ALU.add),
                           incs=[(s["VC"], 1)])
                        mk["abV"][(fsub, e)] = mark(s["VC"])
                        OP("scalar", lambda q, ei=ei: q.activation(
                            se2[ei % 2], se2[ei % 2], AF.Silu),
                           waits=[(s["VC"], mk["abV"][(fsub, e)])], incs=[(s["SC"], 1)])
                        mk["aSC"][(fsub, e)] = mark(s["SC"])
                        OP("gpsimd", lambda q, ei=ei: q.tensor_tensor(
                            act2[ei % 2], se2[ei % 2], be2[ei % 2], ALU.mult),
                           waits=[(s["SC"], mk["aSC"][(fsub, e)]),
                                  (s["TT"], mk["zTT"].get(prev2, 0))] + fw("gpsimd"),
                           incs=[(s["GP"], 1)])
                        mk["actGP"][(fsub, e)] = mark(s["GP"])
                        m_act = mk["actGP"][(fsub, e)]
                        for mc in range(4):
                            OP("tensor", lambda q, e=e, mc=mc, ei=ei: q.matmul(
                                pz2(mc)[:, 16 * e:16 * e + 16],
                                act2[ei % 2][:, mc * 128:(mc + 1) * 128],
                                A2c[:, (ei // 8) * 128 + 16 * e:
                                    (ei // 8) * 128 + 16 * e + 16],
                                start=(ei == 0 and mc == 0),
                                stop=(fsub == 3 and e == 7),
                                skip_group_check=True),
                               waits=([(s["GP"], m_act)] if mc == 0 else []),
                               incs=([(s["TT"], 1)] if mc == 3 else []))
                        mk["zTT"][(fsub, e)] = mark(s["TT"])
                        OP("gpsimd", lambda q, e=e, ei=ei: q.tensor_tensor(
                            u2[ei % 2], act2[ei % 2], wb[:, e * 512:(e + 1) * 512],
                            ALU.mult),
                           waits=[(s["DLX"], m_ld),
                                  (s["VC"], mk["mixV"].get(prev2, 0))],
                           incs=[(s["GP"], 1)])
                        mk["uGP"][(fsub, e)] = mark(s["GP"])
                        if e == 0:
                            OP("vector", lambda q, fsub=fsub, ei=ei: q.tensor_copy(
                                mix[:, fsub * 512:(fsub + 1) * 512], u2[ei % 2]),
                               waits=[(s["GP"], mk["uGP"][(fsub, e)]),
                                      (s["TT"], mk["eDone"])],
                               incs=[(s["VC"], 1)])
                        else:
                            OP("vector", lambda q, fsub=fsub, ei=ei: q.tensor_tensor(
                                mix[:, fsub * 512:(fsub + 1) * 512],
                                mix[:, fsub * 512:(fsub + 1) * 512], u2[ei % 2],
                                ALU.add),
                               waits=[(s["GP"], mk["uGP"][(fsub, e)])], incs=[(s["VC"], 1)])
                        mk["mixV"][(fsub, e)] = mark(s["VC"])
                m_mix = mark(s["VC"])
                mk["wbGP"] = mark(s["GP"])
                prev_stage = 0
                for m_ in range(4):
                    for n in range(2):
                        for k_ in range(4):
                            OP("tensor", lambda q, m_=m_, n=n, k_=k_: q.matmul(
                                pO,
                                mix[:, k_ * 512 + m_ * 128: k_ * 512 + (m_ + 1) * 128],
                                wdT[:, k_ * 1024 + n * 512: k_ * 1024 + (n + 1) * 512],
                                start=(k_ == 0), stop=(k_ == 3)),
                               waits=([(s["VC"], m_mix), (s["SC"], prev_stage)]
                                      if k_ == 0 else []),
                               incs=([(s["TT"], 1)] if k_ == 3 else []))
                        m_o = mark(s["TT"])
                        OP("scalar", lambda q, st=stage_o[m_ % 2], n=n: q.copy(
                            st[:, n * 512:(n + 1) * 512], pO),
                           waits=[(s["TT"], m_o)] +
                                 ([(s["DST"], mk["OdD"][m_ % 2])] if n == 0 else []),
                           incs=[(s["SC"], 1)])
                        prev_stage = mark(s["SC"])
                    i128 = 4 * j + m_
                    OP("gpsimd", lambda q, st=stage_o[m_ % 2], i128=i128: q.dma_start(
                        out=outd[i128 * 128:(i128 + 1) * 128, :], in_=st,
                        accum_op=(ALU.add if fc > 0 else ALU.bypass)),
                       waits=[(s["SC"], prev_stage)], incs=[(s["DST"], 16)])
                    mk["OdD"][m_ % 2] = mark(s["DST"])
                for mc in range(4):
                    OP("scalar", lambda q, mc=mc: q.copy(
                        zscr[:, mc * 128:(mc + 1) * 128], pz2(mc)),
                       waits=([(s["TT"], mk["zTT"][(3, 7)]), (s["DST"], mk["stzD"])]
                              if mc == 0 else []),
                       incs=([(s["SC"], 1)] if mc == 3 else []))
                m_ze = mark(s["SC"])
                for mc in range(4):
                    OP("tensor", lambda q, mc=mc: q.transpose(
                        ps[:, 1024 + mc * 128: 1024 + (mc + 1) * 128],
                        zscr[:, mc * 128:(mc + 1) * 128], idsb[:]),
                       waits=([(s["SC"], m_ze),
                               (s["VC"], mk["abV"][(3, 7)])] if mc == 0 else []),
                       incs=([(s["TT"], 1)] if mc == 3 else []))
                m_ztr = mark(s["TT"])
                OP("scalar", lambda q: q.copy(stz, ps[:, 1024:1536]),
                   waits=[(s["TT"], m_ztr)], incs=[(s["SC"], 1)])
                m_ze2 = mark(s["SC"])
                OP("gpsimd", lambda q, j=j: q.dma_start(
                    out=zd[:, j * 512:(j + 1) * 512], in_=stz,
                    accum_op=(ALU.add if fc > 0 else ALU.bypass)),
                   waits=[(s["SC"], m_ze2)], incs=[(s["DST"], 16)])
                mk["stzD"] = mark(s["DST"])
                mk["eDone"] = mark(s["TT"])

        def phase_G():
            bar = [(nm_, P.v(nm_)) for nm_ in s.values()]
            b2W = WA[:, 0:1024]
            z_s = SMM[:, 0:512]
            z_s2 = SMM[:, 512:1024]
            xm = [SF32[:, 0:1024], SF32[:, 1024:2048]]
            om = [SF32[:, 2048:3072], SF32[:, 3072:4096]]
            tsum = SF32[:, 4096:5120]
            wbz = SF32[:, 5120:5632]
            pO = ps[:, 0:2048]
            OP("sync", lambda q: q.dma_start(out=b2W, in_=W["B2catT"]),
               waits=bar, incs=[(s["DLW"], 16)])
            m_w = mark(s["DLW"])
            mk = dict(RSTD=0, zTT=0, yD=[0, 0], vD=[0, 0], ZS=0, xSC=[0, 0])
            first = {k: [bar] for k in ("vector", "scalar", "gpsimd", "tensor")}

            def fw(k):
                return first[k].pop() if first[k] else []

            for j in range(NTOK // 512):
                OP("sync", lambda q, j=j: q.dma_start(
                    out=z_s, in_=zd[:, j * 512:(j + 1) * 512]),
                   waits=[(s["TT"], mk["zTT"])] + (bar if j == 0 else []),
                   incs=[(s["DLX"], 16)])
                for e in range(8):
                    OP("sync", lambda q, e=e, j=j: q.dma_start(
                        out=wbz[16 * e:16 * e + 16, :],
                        in_=wgtd[e:e + 1, j * 512:(j + 1) * 512].to_broadcast([16, 512])),
                       waits=([(s["VC"], mk["ZS"])] if e == 0 else []),
                       incs=[(s["DLX"], 16)])
                m_z = mark(s["DLX"])
                OP("vector", lambda q: q.tensor_tensor(z_s2, z_s, wbz, ALU.mult),
                   waits=[(s["DLX"], m_z), (s["TT"], mk["zTT"])] + fw("vector"),
                   incs=[(s["VC"], 1)])
                m_zs = mark(s["VC"])
                mk["ZS"] = m_zs
                for m_ in range(4):
                    i128 = 4 * j + m_
                    OP("sync", lambda q, i128=i128, m_=m_: q.dma_start(
                        out=xm[m_ % 2], in_=xtd[i128 * 128:(i128 + 1) * 128, :]),
                       waits=[(s["VC"], mk["vD"][m_ % 2]), (s["SC"], mk["xSC"][m_ % 2])],
                       incs=[(s["DLX"], 16)])
                    OP("sync", lambda q, i128=i128, m_=m_: q.dma_start(
                        out=om[m_ % 2], in_=outd[i128 * 128:(i128 + 1) * 128, :]),
                       waits=[(s["DST"], mk["yD"][m_ % 2])], incs=[(s["DLX"], 16)])
                    m_x = mark(s["DLX"])
                    for n in range(2):
                        OP("tensor", lambda q, n=n, m_=m_: q.matmul(
                            pO[:, (m_ % 2) * 1024 + n * 512:
                               (m_ % 2) * 1024 + (n + 1) * 512],
                            z_s2[:, m_ * 128:(m_ + 1) * 128],
                            b2W[:, n * 512:(n + 1) * 512],
                            start=True, stop=True),
                           waits=(([(s["VC"], m_zs), (s["DLW"], m_w),
                                    (s["VC"], mk["vD"][m_ % 2])] + fw("tensor"))
                                  if n == 0 else []),
                           incs=([(s["TT"], 1)] if n == 1 else []))
                    m_mm = mark(s["TT"])
                    if m_ == 3:
                        mk["zTT"] = m_mm
                    OP("vector", lambda q, m_=m_: q.scalar_tensor_tensor(
                        tsum, pO[:, (m_ % 2) * 1024:(m_ % 2) * 1024 + 1024], 2.0,
                        om[m_ % 2], ALU.mult, ALU.add),
                       waits=[(s["TT"], m_mm), (s["DLX"], m_x), (s["SC"], mk["RSTD"])]
                             + fw("vector"))
                    OP("vector", lambda q, m_=m_: q.tensor_tensor(
                        xm[m_ % 2], tsum, xm[m_ % 2], ALU.add), incs=[(s["VC"], 1)])
                    OP("vector", lambda q, m_=m_: q.reduce_sum(
                        st1[:], xm[m_ % 2], axis=AX.X, negate=True), incs=[(s["VC"], 1)])
                    m_sum = mark(s["VC"])
                    OP("scalar", lambda q: q.mul(nm1[:], st1[:], 1.0 / 1024),
                       waits=[(s["VC"], m_sum)] + fw("scalar"))
                    OP("scalar", lambda q, m_=m_: q.activation(
                        tsum, xm[m_ % 2], AF.Square, bias=st1[:], scale=1024.0,
                        accum_out=sq1[:]))
                    OP("scalar", lambda q: q.activation(lv1[:], sq1[:], AF.Ln, bias=eps_sb[:],
                                                        scale=1.0 / (1024.0 ** 3)))
                    OP("scalar", lambda q: q.activation(rstd1[:], lv1[:], AF.Exp, scale=-0.5),
                       incs=[(s["SC"], 1)])
                    mk["RSTD"] = mark(s["SC"])
                    mk["xSC"][m_ % 2] = mk["RSTD"]
                    OP("vector", lambda q, m_=m_: q.tensor_scalar(
                        om[m_ % 2], xm[m_ % 2], nm1[:], rstd1[:], ALU.add, ALU.mult),
                       waits=[(s["SC"], mk["RSTD"]), (s["DST"], mk["yD"][m_ % 2])],
                       incs=[(s["VC"], 1)])
                    OP("vector", lambda q, m_=m_: q.tensor_copy(
                        vfen[:], om[m_ % 2][:, 1016:1024]), incs=[(s["VC"], 1)])
                    mk["vD"][m_ % 2] = mark(s["VC"])
                    OP("gpsimd", lambda q, i128=i128, m_=m_: q.dma_start(
                        out=y_out[i128 * 128:(i128 + 1) * 128, :], in_=om[m_ % 2]),
                       waits=[(s["VC"], mk["vD"][m_ % 2])] + fw("gpsimd"),
                       incs=[(s["DST"], 16)])
                    mk["yD"][m_ % 2] = mark(s["DST"])

        self.phase_T = phase_T
        self.phase_F = phase_F
        self.phase_G = phase_G

        phases = self.cfg.get("phases", ["A"])
        for l in range(nl):
            if "A" in phases:
                phase_A(l)
            if "B" in phases:
                phase_B(l)
            if "C" in phases:
                phase_C(l, 0)
                phase_C(l, 2048)
        if "T" in phases:
            phase_T()
        if "F" in phases:
            for fc in range(8):
                phase_F(fc)
        if "G" in phases:
            phase_G()

        OP("gpsimd", lambda q: q.nop(),
           waits=[(s["DST"], P.v(s["DST"])), (s["DLX"], P.v(s["DLX"])),
                  (s["DLW"], P.v(s["DLW"]))])

        block = self.ctx.enter_context(nc.Block())
        P.emit(block, self.handles)
        self.ctx.close()
        return nc


from concourse.bass_utils import run_bass_kernel_spmd


def kernel(**inputs):
    in_maps = shard_inputs(inputs, 8)
    m = MK(dict(f32r=True, dev=False, nlayers=6, T=4096, nseq=2,
                phases=["A", "B", "C", "T", "F", "G"]))
    nc = m.build()
    res = run_bass_kernel_spmd(nc, in_maps, core_ids=list(range(8)))
    return np.concatenate([r["y_out"].reshape(2, 4096, 1024) for r in res.results],
                          axis=0)
